# version 11
# baseline (speedup 1.0000x reference)
import contextlib

import numpy as np
import ml_dtypes

import concourse.bass as bass
import concourse.mybir as mybir
from concourse.bass_utils import run_bass_kernel_spmd

F32 = mybir.dt.float32
BF16 = mybir.dt.bfloat16
AF = mybir.ActivationFunctionType
ALU = mybir.AluOpType
AX = mybir.AxisListType
BF = ml_dtypes.bfloat16

NCORES = 8
N = 4096
DM = 1024
NT = N // 128
DC = DM // 128
NQR = 8
QR = 512


class Prog:

    def __init__(self, nc, es):
        self.nc = nc
        self.es = es
        self.sems = {}
        self.cnt = {}

    def sem(self, name):
        if name not in self.sems:
            self.sems[name] = self.es.enter_context(self.nc.semaphore(name))
            self.cnt[name] = 0
        return self.sems[name]

    def inc(self, instr, name, k=1):
        instr.then_inc(self.sem(name), k)
        self.cnt[name] += k
        return self.cnt[name]

    def tick(self, name):
        self.sem(name)
        return self.cnt[name]

    def finc(self, instr, eng, name):
        f = "_f" + type(eng).__name__[-12:]
        instr.then_inc(self.sem(f), 1)
        self.cnt[f] += 1
        eng.wait_ge(self.sem(f), self.cnt[f])
        eng.nop().then_inc(self.sem(name), 1)
        self.cnt[name] += 1
        return self.cnt[name]

    def dma(self, q, out, in_, name, **kw):
        instr = q.dma_start(out=out, in_=in_, **kw)
        instr.then_inc(self.sem(name), 16)
        self.cnt[name] += 16
        return self.cnt[name]

    def wait(self, eng, name, val=None):
        if val is None:
            val = self.cnt[name]
        if val > 0:
            eng.wait_ge(self.sem(name), val)


class Rot:

    def __init__(self, n):
        self.n = n
        self.prod = [None] * n
        self.cons = [None] * n
        self.idx = 0

    def next(self):
        s = self.idx % self.n
        self.idx += 1
        return s


def guard(p, eng, rec):
    if rec is not None:
        name, tick = rec
        p.wait(eng, name, tick)



def emit_ln(p, block, X_ap, xnT, c, tag):
    nc = p.nc
    ld, st, xn_s, tr_s, ev = (tag + k for k in ("ld", "st", "xn", "tr", "ev"))
    base = {k: p.tick(k) for k in (ld, st, xn_s, tr_s, ev)}

    def f_eps(v):
        p.finc(v.memset(c["ln_eps"][:], 1e-5), v, tag + "ep")
    block.vector(f_eps)

    for i in range(NT):
        bi = i % 2

        def f_load(q, i=i, bi=bi):
            if i >= 2:
                p.wait(q, xn_s, base[xn_s] + i - 1)
            p.dma(q, c["ln_xt"][bi][:], X_ap[i * 128:(i + 1) * 128, :], ld)
        block.sync(f_load)

        def f_sums(v, i=i, bi=bi):
            def rt_(instr):
                n = p.inc(instr, tag + "vr")
                p.wait(v, tag + "vr", n)
            p.wait(v, ld, base[ld] + 16 * (i + 1))
            if i >= 2:
                p.wait(v, xn_s, base[xn_s] + i - 1)
            sq = c["ln_sq"][bi]
            v.tensor_scalar(sq[:], c["ln_xt"][bi][:], 1.0, 0.0, ALU.mult, ALU.add,
                            accum_out=c["ln_sm"][bi][:])
            rt_(v.scalar_tensor_tensor(sq[:], c["ln_xt"][bi][:], 1.0,
                                       c["ln_xt"][bi][:], ALU.mult, ALU.mult,
                                       accum_out=c["ln_sq2"][bi][:]))
            rt_(v.tensor_scalar(c["ln_nm"][bi][:], c["ln_sm"][bi][:], -1.0 / DM, None,
                                ALU.mult, ALU.bypass))
            p.finc(v.scalar_tensor_tensor(c["ln_b6"][bi][:], c["ln_nm"][bi][:], -1.0,
                                          c["ln_nm"][bi][:], ALU.mult, ALU.mult),
                   v, st)
        block.vector(f_sums)

        def f_act(s, i=i, bi=bi):
            def rt_(instr):
                n = p.inc(instr, tag + "op")
                p.wait(s, tag + "op", n)
            p.wait(s, st, base[st] + i + 1)
            if i == 0:
                p.wait(s, tag + "ep", p.tick(tag + "ep"))
            if i >= 2:
                p.wait(s, tr_s, base[tr_s] + (i - 1) * DC)
            rt_(s.activation(c["ln_m2"][bi][:], c["ln_b6"][bi][:], AF.Identity,
                             bias=c["ln_eps"][:]))
            s.activation(c["ln_l"][bi][:], c["ln_sq2"][bi][:], AF.Ln,
                         scale=1.0 / DM, bias=c["ln_m2"][bi][:])
            rt_(s.activation(c["ln_r"][bi][:], c["ln_l"][bi][:], AF.Exp, scale=-0.5))
            rt_(s.activation(c["ln_m"][bi][:], c["ln_nm"][bi][:], AF.Identity,
                             scale=c["ln_r"][bi][:]))
            instr = s.activation(c["ln_xn"][bi][:], c["ln_xt"][bi][:], AF.Identity,
                                 bias=c["ln_m"][bi][:], scale=c["ln_r"][bi][:])
            p.finc(instr, s, xn_s)
        block.scalar(f_act)

        for fc in range(DC):
            k = i * DC + fc
            ps = c["ln_trps"][k % 2]

            def f_tr(t, i=i, bi=bi, fc=fc, k=k, ps=ps):
                if fc == 0:
                    p.wait(t, xn_s, base[xn_s] + i + 1)
                if k >= 2:
                    p.wait(t, ev, base[ev] + k - 1)
                p.inc(t.transpose(ps[:], c["ln_xn"][bi][:, fc * 128:(fc + 1) * 128],
                                  c["ident"][:]), tr_s)
            block.tensor(f_tr)

            def f_ev(v, i=i, fc=fc, k=k, ps=ps):
                p.wait(v, tr_s, base[tr_s] + k + 1)
                instr = v.tensor_copy(xnT[:, fc * N + i * 128: fc * N + (i + 1) * 128], ps[:])
                p.inc(instr, ev)
            block.vector(f_ev)


def alloc_ln_common(p, sb, ps):
    c = {}
    c["ln_xt"] = [sb("ln_xt%d" % k, [128, DM], F32) for k in range(2)]
    c["ln_sq"] = [sb("ln_sq%d" % k, [128, DM], BF16) for k in range(2)]
    for nm in ("ln_sm", "ln_sq2", "ln_nm", "ln_m2", "ln_b6", "ln_l", "ln_r", "ln_m"):
        c[nm] = [sb(nm + "%d" % k, [128, 1], F32) for k in range(2)]
    c["ln_xn"] = [sb("ln_xn%d" % k, [128, DM], BF16) for k in range(2)]
    c["ln_trps"] = [ps("ln_trps%d" % k, [128, 128], BF16) for k in range(2)]
    c["ln_eps"] = sb("ln_eps", [128, 1], F32)
    return c



def emit_p4(p, block, es, T, Xin_ap, Xout_ap):
    nc = p.nc

    def sb(name, shape, dt):
        return es.enter_context(nc.sbuf_tensor("p4_" + name, shape, dt))

    def ps(name, shape, dt):
        return es.enter_context(nc.psum_tensor("p4_" + name, shape, dt))

    c = alloc_ln_common(p, sb, ps)
    c["ident"] = T["ident_bf"]

    xnT = sb("xnT", [128, DC * N], BF16)
    u1s = sb("u1s", [128, 8 * 256], BF16)
    v1s = sb("v1s", [128, 2 * 4096], BF16)
    u2s = sb("u2s", [128, 32 * 256], BF16)
    v2s = sb("v2s", [128, 2 * 1024], BF16)
    b1c = sb("b1c", [128, 32], F32)
    b2r = sb("b2r", [1, DM], F32)
    b2rb = sb("b2rb", [1, DM], BF16)
    b1l = sb("b1l", [32, 128], F32)
    ptb = sb("ptb", [128, 2 * QR], BF16)
    hsb = sb("hsb", [128, 32 * QR], BF16)
    qtb = sb("qtb", [128, 2 * QR], BF16)
    nx = [sb("nx%d" % k, [128, DM], F32) for k in range(2)]
    xt4 = [sb("xt4%d" % k, [128, DM], F32) for k in range(2)]

    pp = [ps("pp%d" % k, [128, QR], F32) for k in range(2)]
    po = [ps("po%d" % k, [128, QR], F32) for k in range(2)]
    pb1 = ps("pb1", [128, 32], F32)

    rot_pp = Rot(2)
    rot_po = Rot(2)

    def f_cast(g):
        for wname in ("u1", "v1", "u2", "v2"):
            p.dma(g, T[wname + "_bf"], T[wname], "p4wc")
    block.gpsimd(f_cast)

    def f_wload(q):
        p.wait(q, "p4wc", 16)
        p.dma(q, u1s[:].rearrange("p (c r) -> p c r", c=8), T["u1_bf"].rearrange("(c p) r -> p c r", p=128), "p4wl")
        p.wait(q, "p4wc", 32)
        p.dma(q, v1s[:].rearrange("p (c f) -> p c f", c=2), T["v1_bf"].rearrange("(c p) f -> p c f", p=128), "p4wl")
        p.wait(q, "p4wc", 48)
        p.dma(q, u2s[:].rearrange("p (c r) -> p c r", c=32), T["u2_bf"].rearrange("(c p) r -> p c r", p=128), "p4wl")
        p.wait(q, "p4wc", 64)
        p.dma(q, v2s[:].rearrange("p (c d) -> p c d", c=2), T["v2_bf"].rearrange("(c p) d -> p c d", p=128), "p4wl")
        p.dma(q, b1l[:], T["b1"].rearrange("(c p) -> c p", p=128), "p4wl")
        p.dma(q, b2r[:], T["b2"].rearrange("(o d) -> o d", o=1), "p4wl")
    block.sync(f_wload)

    def f_b1tr(t):
        p.wait(t, "p4wl", p.tick("p4wl"))
        p.inc(t.transpose(pb1[:], b1l[:], T["ident_f32"][0:32, 0:32]), "p4bt")
    block.tensor(f_b1tr)

    def f_bev(v):
        p.wait(v, "p4bt", 1)
        v.tensor_copy(b1c[:], pb1[:])
        p.finc(v.tensor_copy(b2rb[:], b2r[:]), v, "p4be")
    block.vector(f_bev)

    emit_ln(p, block, Xin_ap, xnT, c, "ln")

    ln_done = p.tick("lnev")
    wl_done = p.tick("p4wl")

    for qr in range(NQR):
        for rc in range(2):
            s = rot_pp.next()

            def f_mm(t, qr=qr, rc=rc, s=s):
                if qr == 0 and rc == 0:
                    p.wait(t, "lnev", ln_done)
                    p.wait(t, "p4wl", wl_done)
                guard(p, t, rot_pp.cons[s])
                for dc in range(DC):
                    instr = t.matmul(pp[s][:], u1s[:, dc * 256 + rc * 128: dc * 256 + rc * 128 + 128],
                                     xnT[:, dc * N + qr * QR: dc * N + (qr + 1) * QR],
                                     start=(dc == 0), stop=(dc == DC - 1))
                p.inc(instr, "p4pe")
            block.tensor(f_mm)
            rot_pp.prod[s] = ("p4pe", p.tick("p4pe"))

            def f_ev(v, qr=qr, rc=rc, s=s):
                guard(p, v, rot_pp.prod[s])
                p.finc(v.tensor_copy(ptb[:, rc * QR:(rc + 1) * QR], pp[s][:]), v, "p4ev")
            block.vector(f_ev)
            rot_pp.cons[s] = ("p4ev", p.tick("p4ev"))

        pt_ready = p.tick("p4ev")

        ge_jobs = []
        base_ge = p.tick("p4ge")
        for fc in range(32):
            s = rot_pp.next()

            def f_mm(t, qr=qr, fc=fc, s=s, pt_ready=pt_ready):
                if fc == 0:
                    p.wait(t, "p4ev", pt_ready)
                guard(p, t, rot_pp.cons[s])
                for rc in range(2):
                    instr = t.matmul(pp[s][:], v1s[:, rc * 4096 + fc * 128: rc * 4096 + fc * 128 + 128],
                                     ptb[:, rc * QR:(rc + 1) * QR],
                                     start=(rc == 0), stop=(rc == 1))
                p.inc(instr, "p4pe")
            block.tensor(f_mm)
            rot_pp.prod[s] = ("p4pe", p.tick("p4pe"))

            ge_jobs.append((fc, s, rot_pp.prod[s], qr == 0 and fc == 0))
            p.cnt["p4ge"] = base_ge + len(ge_jobs)
            rot_pp.cons[s] = ("p4ge", p.cnt["p4ge"])

        def f_gels(sc, jobs=tuple(ge_jobs), base_ge=base_ge):
            saved = p.cnt["p4ge"]
            p.cnt["p4ge"] = base_ge
            for fc, s, prod, first in jobs:
                guard(p, sc, prod)
                if first:
                    p.wait(sc, "p4be", 1)
                p.finc(sc.activation(hsb[:, fc * QR:(fc + 1) * QR], pp[s][:],
                                     AF.Gelu_apprx_tanh, bias=b1c[:, fc:fc + 1]), sc, "p4ge")
            assert p.cnt["p4ge"] == saved
        block.scalar(f_gels)

        h_ready = p.tick("p4ge")

        for rc in range(2):
            s = rot_pp.next()

            def f_mm(t, qr=qr, rc=rc, s=s, h_ready=h_ready):
                if rc == 0:
                    p.wait(t, "p4ge", h_ready)
                guard(p, t, rot_pp.cons[s])
                for fc in range(32):
                    instr = t.matmul(pp[s][:], u2s[:, fc * 256 + rc * 128: fc * 256 + rc * 128 + 128],
                                     hsb[:, fc * QR:(fc + 1) * QR],
                                     start=(fc == 0), stop=(fc == 31))
                p.inc(instr, "p4pe")
            block.tensor(f_mm)
            rot_pp.prod[s] = ("p4pe", p.tick("p4pe"))

            def f_ev(v, qr=qr, rc=rc, s=s):
                guard(p, v, rot_pp.prod[s])
                p.finc(v.tensor_copy(qtb[:, rc * QR:(rc + 1) * QR], pp[s][:]), v, "p4ev")
            block.vector(f_ev)
            rot_pp.cons[s] = ("p4ev", p.tick("p4ev"))

        qt_ready = p.tick("p4ev")

        for tsub in range(4):
            ti = qr * 4 + tsub
            bx = tsub % 2

            def f_xld(q, ti=ti, bx=bx):
                if ti >= 2:
                    p.wait(q, "p4st", 16 * (ti - 1))
                p.dma(q, xt4[bx][:], Xin_ap[ti * 128:(ti + 1) * 128, :], "p4xl")
            block.sync(f_xld)

            for nh in range(2):
                s = rot_po.next()

                def f_mm(t, qr=qr, tsub=tsub, nh=nh, s=s, qt_ready=qt_ready):
                    if tsub == 0 and nh == 0:
                        p.wait(t, "p4ev", qt_ready)
                    guard(p, t, rot_po.cons[s])
                    for rc in range(2):
                        t.matmul(po[s][:], qtb[:, rc * QR + tsub * 128: rc * QR + tsub * 128 + 128],
                                 v2s[:, rc * DM + nh * 512: rc * DM + (nh + 1) * 512],
                                 start=(rc == 0), stop=False)
                    instr = t.matmul(po[s][:], T["ones_big"][0:1, 0:128],
                                     b2rb[:, nh * 512:(nh + 1) * 512],
                                     start=False, stop=True)
                    p.inc(instr, "p4pe")
                block.tensor(f_mm)
                rot_po.prod[s] = ("p4pe", p.tick("p4pe"))

                def f_res(v, ti=ti, bx=bx, nh=nh, s=s):
                    guard(p, v, rot_po.prod[s])
                    p.wait(v, "p4xl", 16 * (ti + 1))
                    p.inc(v.scalar_tensor_tensor(nx[bx][:, nh * 512:(nh + 1) * 512], po[s][:],
                                                 1.0, xt4[bx][:, nh * 512:(nh + 1) * 512],
                                                 ALU.mult, ALU.add), "p4rs")
                block.vector(f_res)
                rot_po.cons[s] = ("p4rs", p.tick("p4rs"))

            def f_st(q, ti=ti, bx=bx):
                p.wait(q, "p4rs", p.tick("p4rs"))
                p.dma(q, Xout_ap[ti * 128:(ti + 1) * 128, :], nx[bx][:], "p4st")
            block.sync(f_st)

    def f_fin(g):
        p.wait(g, "p4st", p.tick("p4st"))
    block.gpsimd(f_fin)





def emit_p1(p, block, es, T, Xin_ap, Xout_ap, dbg=None):
    nc = p.nc
    EPS = 1e-6
    H = 16
    NP = H // 2

    def sb(name, shape, dt):
        return es.enter_context(nc.sbuf_tensor("p1_" + name, shape, dt))

    xnT = sb("xnT", [128, DC * N], BF16)
    wvs = sb("wvs", [128, DC * DM], BF16)
    wos2 = sb("wos2", [128, DC * DM], BF16)
    wkp = sb("wkp", [128, H * DC * 64], BF16)
    wqp = sb("wqp", [128, H * DC * 64], BF16)
    projs2 = sb("projs2", [128, 64], BF16)
    proj32 = sb("proj32", [64, 64], F32)
    wtb = [sb("wtb%d" % k, [128, DM], BF16) for k in range(2)]
    kvall = sb("kvall", [128, NP * 64], BF16)
    ksall = sb("ksall", [128, NP], BF16)
    vfull = [sb("vfull%d" % k, [128, DM], BF16) for k in range(2)]
    kfc = [sb("kfc%d" % k, [128, 64], BF16) for k in range(2)]
    qfb = [sb("qfb%d" % k, [128, QR], BF16) for k in range(2)]
    cat = sb("cat", [128, NP * QR], BF16)
    den_sc = sb("den_sc", [128, QR], F32)
    rb_bf = sb("rb_bf", [128, QR], BF16)
    rb2 = sb("rb2", [128, QR], BF16)
    eps6 = sb("eps6", [128, 1], F32)
    bo32 = sb("bo32", [1, DM], F32)
    bor = sb("bor", [1, DM], BF16)
    nx = [sb("nx%d" % k, [128, DM], F32) for k in range(2)]
    xt1 = [sb("xt1%d" % k, [128, DM], F32) for k in range(2)]

    es_a = contextlib.ExitStack()
    kfps = [es_a.enter_context(nc.psum_tensor("p1_kfps%d" % k, [128, 64], F32)) for k in range(2)]

    def f_cast(g):
        for wname in ("wq", "wk", "wv", "wo", "proj"):
            p.dma(g, T[wname + "_bf"], T[wname], "p1wc")
    block.gpsimd(f_cast)

    def f_wload(q):
        p.wait(q, "p1wc", 48)
        p.dma(q, wvs[:].rearrange("p (c o) -> p c o", c=DC),
              T["wv_bf"].rearrange("(c p) o -> p c o", p=128), "p1wl")
        p.wait(q, "p1wc", 64)
        p.dma(q, wos2[:].rearrange("p (c o) -> p c o", c=DC),
              T["wo_bf"].rearrange("(c p) o -> p c o", p=128), "p1wl")
        p.dma(q, proj32[:], T["proj"], "p1wl")
        p.dma(q, bo32[:], T["bo"].rearrange("(o d) -> o d", o=1), "p1wl")
    block.sync(f_wload)

    def f_pcast(v):
        p.wait(v, "p1wl", p.tick("p1wl"))
        v.tensor_copy(projs2[0:64, :], proj32[:])
        v.memset(eps6[:], EPS)
        p.finc(v.tensor_copy(bor[:], bo32[:]), v, "p1pc")
    block.vector(f_pcast)

    def f_pdup(g):
        p.wait(g, "p1pc", 1)
        p.dma(g, projs2[64:128, :], proj32[:], "p1pd")
    block.gpsimd(f_pdup)

    rot_wt = Rot(2)
    rot_kf = Rot(2)
    for wi, (wbf_name, dst) in enumerate((("wq_bf", wqp), ("wk_bf", wkp))):
        for hp in range(NP):
            s = rot_wt.next()

            def f_tl(sc, wi=wi, hp=hp, s=s):
                if wi == 0 and hp == 0:
                    p.wait(sc, "p1wc", 32)
                    p.wait(sc, "p1pd", 16)
                guard(p, sc, rot_wt.cons[s])
                p.dma(sc, wtb[s][:], T[wbf_name][:, hp * 128:(hp + 1) * 128], "p1tl",
                      transpose=True)
            block.scalar(f_tl)
            rot_wt.prod[s] = ("p1tl", p.tick("p1tl"))

            for par in range(2):
                h = hp * 2 + par
                for dcc in range(DC):
                    ks = rot_kf.next()

                    def f_mm(t, wi=wi, hp=hp, par=par, s=s, dcc=dcc, ks=ks):
                        if par == 0 and dcc == 0:
                            guard(p, t, rot_wt.prod[s])
                            if wi == 0 and hp == 0:
                                p.wait(t, "p1pc", 1)
                                p.wait(t, "p1pd", 16)
                        guard(p, t, rot_kf.cons[ks])
                        p.inc(t.matmul(kfps[ks][:],
                                       wtb[s][par * 64:(par + 1) * 64, dcc * 128:(dcc + 1) * 128],
                                       projs2[par * 64:(par + 1) * 64, :],
                                       start=True, stop=True), "p1pe")
                    block.tensor(f_mm)
                    rot_kf.prod[ks] = ("p1pe", p.tick("p1pe"))

                    def f_ev(v, dst=dst, h=h, dcc=dcc, ks=ks):
                        guard(p, v, rot_kf.prod[ks])
                        p.inc(v.tensor_copy(dst[:, (h * DC + dcc) * 64:(h * DC + dcc + 1) * 64],
                                            kfps[ks][:]), "p1pv")
                    block.vector(f_ev)
                    rot_kf.cons[ks] = ("p1pv", p.tick("p1pv"))
            rot_wt.cons[s] = ("p1pe", p.tick("p1pe"))

    es_ln = contextlib.ExitStack()

    def sbl(name, shape, dt):
        return es_ln.enter_context(nc.sbuf_tensor("p1_" + name, shape, dt))

    def psl(name, shape, dt):
        return es_ln.enter_context(nc.psum_tensor("p1_" + name, shape, dt))

    c = alloc_ln_common(p, sbl, psl)
    c["ident"] = T["ident_bf"]
    emit_ln(p, block, Xin_ap, xnT, c, "ln")
    ln_done = p.tick("lnev")
    es_ln.close()

    vps = es_a.enter_context(nc.psum_tensor("p1_vps", [128, DM], F32))
    kvps = es_a.enter_context(nc.psum_tensor("p1_kvps", [128, NP * 64], F32))
    ksps = es_a.enter_context(nc.psum_tensor("p1_ksps", [128, NP], F32))

    pre_done = p.tick("p1pv")

    rot_vf = Rot(2)
    rot_kc = Rot(2)
    vev_tick = None
    for ci in range(NT):
        s = rot_vf.next()

        def f_vmm(t, ci=ci, s=s):
            if ci == 0:
                p.wait(t, "lnev", ln_done)
                p.wait(t, "p1pv", pre_done)
            if vev_tick is not None:
                p.wait(t, "p1vv", vev_tick)
            for half in range(2):
                for dcc in range(DC):
                    p.inc(t.matmul(vps[:, half * 512:(half + 1) * 512],
                                   xnT[:, dcc * N + ci * 128:dcc * N + (ci + 1) * 128],
                                   wvs[:, dcc * DM + half * 512:dcc * DM + (half + 1) * 512],
                                   start=(dcc == 0), stop=(dcc == DC - 1)), "p1pe")
        block.tensor(f_vmm)
        vmm_tick = p.tick("p1pe")

        def f_vev(v, ci=ci, s=s, vmm_tick=vmm_tick):
            p.wait(v, "p1pe", vmm_tick)
            guard(p, v, rot_vf.cons[s])
            p.finc(v.tensor_copy(vfull[s][:], vps[:]), v, "p1vv")
        block.vector(f_vev)
        vev_tick = p.tick("p1vv")
        rot_vf.prod[s] = ("p1vv", vev_tick)

        for h in range(H):
            hp, par = h // 2, h % 2
            b = par * 64
            ks = rot_kf.next()
            kcs = rot_kc.next()

            def f_kfmm(t, ci=ci, h=h, ks=ks):
                guard(p, t, rot_kf.cons[ks])
                for dcc in range(DC):
                    instr = t.matmul(kfps[ks][:],
                                     xnT[:, dcc * N + ci * 128:dcc * N + (ci + 1) * 128],
                                     wkp[:, (h * DC + dcc) * 64:(h * DC + dcc + 1) * 64],
                                     start=(dcc == 0), stop=(dcc == DC - 1))
                p.inc(instr, "p1pe")
            block.tensor(f_kfmm)
            rot_kf.prod[ks] = ("p1pe", p.tick("p1pe"))

            def f_kfev(v, ks=ks, kcs=kcs):
                guard(p, v, rot_kf.prod[ks])
                guard(p, v, rot_kc.cons[kcs])
                p.finc(v.tensor_scalar(kfc[kcs][:], kfps[ks][:], EPS, None,
                                       ALU.max, ALU.bypass), v, "p1kv")
            block.vector(f_kfev)
            kf_ev = p.tick("p1kv")
            rot_kf.cons[ks] = ("p1kv", kf_ev)
            rot_kc.prod[kcs] = ("p1kv", kf_ev)

            def f_kvmm(t, ci=ci, h=h, hp=hp, par=par, b=b, s=s, kcs=kcs, kf_ev=kf_ev):
                p.wait(t, "p1kv", kf_ev)
                guard(p, t, rot_vf.prod[s])
                tp = (0, b)
                t.matmul(kvps[b:b + 64, hp * 64:(hp + 1) * 64], kfc[kcs][:],
                         vfull[s][:, h * 64:(h + 1) * 64],
                         start=(ci == 0 and h < 2), stop=(ci == NT - 1),
                         tile_position=tp, skip_group_check=True)
                p.inc(t.matmul(ksps[b:b + 64, hp:hp + 1], kfc[kcs][:],
                               T["ones_big"][:, 0:1],
                               start=(ci == 0 and h < 2), stop=(ci == NT - 1),
                               tile_position=tp, skip_group_check=True), "p1pe")
            block.tensor(f_kvmm)
            kv_tick = p.tick("p1pe")
            rot_kc.cons[kcs] = ("p1pe", kv_tick)
        rot_vf.cons[s] = ("p1pe", kv_tick)

    p1_mm_done = p.tick("p1pe")

    if dbg is not None:
        def f_dbg0(g):
            p.wait(g, "p1pv", pre_done)
            p.dma(g, dbg["wqp"], wqp[:], "dbgs")
            p.dma(g, dbg["wkp"], wkp[:], "dbgs")
        block.gpsimd(f_dbg0)

    def f_kvev(v):
        p.wait(v, "p1pe", p1_mm_done)
        v.tensor_copy(kvall[:], kvps[:])
        p.finc(v.tensor_copy(ksall[:], ksps[:]), v, "p1ka")
    block.vector(f_kvev)
    kva_done = p.tick("p1ka")

    if dbg is not None:
        def f_dbg1(g):
            p.wait(g, "p1ka", kva_done)
            p.dma(g, dbg["kvall"], kvall[:], "dbgs")
            p.dma(g, dbg["ksall"], ksall[:], "dbgs")
            p.dma(g, dbg["xnT"], xnT[:, 0:4096], "dbgs")
        block.gpsimd(f_dbg1)

    es_a.close()

    es_b = contextlib.ExitStack()
    qfps = [es_b.enter_context(nc.psum_tensor("p1_qfps%d" % k, [128, QR], F32)) for k in range(2)]
    nps = [es_b.enter_context(nc.psum_tensor("p1_nps%d" % k, [128, QR], F32)) for k in range(2)]
    dps = es_b.enter_context(nc.psum_tensor("p1_dps", [128, QR], F32))
    rbps = es_b.enter_context(nc.psum_tensor("p1_rbps", [128, QR], F32))
    ops = [es_b.enter_context(nc.psum_tensor("p1_ops%d" % k, [128, QR], F32)) for k in range(2)]

    rot_qf = Rot(2)
    rot_qb = Rot(2)
    rot_n = Rot(2)
    rot_o = Rot(2)
    dps_cons = None
    rbps_cons = None
    rb_bf_cons = None
    for qr in range(NQR):
        den_jobs = []
        base_dn = p.tick("p1dn")
        for hp in range(NP):
            s = rot_qf.next()
            qs = rot_qb.next()

            def f_qfmm(t, qr=qr, hp=hp, s=s):
                if qr == 0 and hp == 0:
                    p.wait(t, "p1ka", kva_done)
                guard(p, t, rot_qf.cons[s])
                for par in range(2):
                    h = hp * 2 + par
                    b = par * 64
                    for dcc in range(DC):
                        instr = t.matmul(qfps[s][b:b + 64, :],
                                         wqp[:, (h * DC + dcc) * 64:(h * DC + dcc + 1) * 64],
                                         xnT[:, dcc * N + qr * QR:dcc * N + (qr + 1) * QR],
                                         start=(dcc == 0), stop=(dcc == DC - 1),
                                         tile_position=(0, b), skip_group_check=True)
                p.inc(instr, "p1pe")
            block.tensor(f_qfmm)
            rot_qf.prod[s] = ("p1pe", p.tick("p1pe"))

            def f_qfev(v, s=s, qs=qs):
                guard(p, v, rot_qf.prod[s])
                guard(p, v, rot_qb.cons[qs])
                p.finc(v.tensor_scalar(qfb[qs][:], qfps[s][:], EPS, None,
                                       ALU.max, ALU.bypass), v, "p1qv")
            block.vector(f_qfev)
            qf_ev = p.tick("p1qv")
            rot_qf.cons[s] = ("p1qv", qf_ev)
            rot_qb.prod[qs] = ("p1qv", qf_ev)

            ns = rot_n.next()

            def f_nmm(t, hp=hp, qs=qs, ns=ns, qf_ev=qf_ev):
                p.wait(t, "p1qv", qf_ev)
                for rec in rot_n.cons[ns] or []:
                    guard(p, t, rec)
                guard(p, t, dps_cons)
                for par in range(2):
                    b = par * 64
                    t.matmul(nps[ns][b:b + 64, :], kvall[b:b + 64, hp * 64:(hp + 1) * 64],
                             qfb[qs][b:b + 64, :], start=True, stop=True,
                             tile_position=(b, b), skip_group_check=True)
                    instr = t.matmul(dps[b:b + 1, :], ksall[b:b + 64, hp:hp + 1],
                                     qfb[qs][b:b + 64, :], start=True, stop=True,
                                     tile_position=(b, b), skip_group_check=True)
                p.inc(instr, "p1pe")
            block.tensor(f_nmm)
            n_tick = p.tick("p1pe")
            rot_n.prod[ns] = ("p1pe", n_tick)
            rot_qb.cons[qs] = ("p1pe", n_tick)

            den_jobs.append((n_tick, rb_bf_cons))
            dn_tick = base_dn + len(den_jobs)
            p.cnt["p1dn"] = dn_tick
            dps_cons = ("p1dn", dn_tick)

            def f_bc(t, dn_tick=dn_tick):
                p.wait(t, "p1dn", dn_tick)
                guard(p, t, rbps_cons)
                for par in range(2):
                    b = par * 64
                    instr = t.matmul(rbps[b:b + 64, :], T["ones_big"][b:b + 1, 0:64],
                                     rb_bf[b:b + 1, :], start=True, stop=True,
                                     tile_position=(b, b), skip_group_check=True)
                p.inc(instr, "p1pe")
            block.tensor(f_bc)
            bc_tick = p.tick("p1pe")
            rb_bf_cons = ("p1pe", bc_tick)

            def f_rbev(v, bc_tick=bc_tick):
                p.wait(v, "p1pe", bc_tick)
                p.inc(v.tensor_copy(rb2[:], rbps[:]), "p1rb")
            block.vector(f_rbev)
            rbps_cons = ("p1rb", p.tick("p1rb"))

            def f_cat(v, hp=hp, ns=ns):
                p.wait(v, "p1rb", p.tick("p1rb"))
                p.inc(v.scalar_tensor_tensor(cat[:, hp * QR:(hp + 1) * QR], nps[ns][:],
                                             1.0, rb2[:], ALU.mult, ALU.mult), "p1ct")
            block.vector(f_cat)
            rot_n.cons[ns] = [("p1ct", p.tick("p1ct"))]

        def f_dens(sc, jobs=tuple(den_jobs), base_dn=base_dn):
            saved = p.cnt["p1dn"]
            p.cnt["p1dn"] = base_dn
            for n_tick, rbg in jobs:
                p.wait(sc, "p1pe", n_tick)
                guard(p, sc, rbg)
                for b in (0, 64):
                    sc.activation(den_sc[b:b + 1, :], dps[b:b + 1, :], AF.Ln,
                                  bias=eps6[b:b + 1, :])
                    instr = sc.activation(rb_bf[b:b + 1, :], den_sc[b:b + 1, :], AF.Exp,
                                          scale=-1.0)
                p.finc(instr, sc, "p1dn")
            assert p.cnt["p1dn"] == saved
        block.scalar(f_dens)

        cat_done = p.tick("p1ct")

        if dbg is not None and qr == 0:
            def f_dbg2(g):
                p.wait(g, "p1ct", cat_done)
                p.dma(g, dbg["cat"], cat[:], "dbgs")
                p.dma(g, dbg["qfb"], qfb[(NQR * NP) % 2 if False else 1][:], "dbgs")
                p.dma(g, dbg["rb2"], rb2[:], "dbgs")
            block.gpsimd(f_dbg2)

        for tsub in range(4):
            ti = qr * 4 + tsub
            bx = tsub % 2

            def f_xld(q, ti=ti, bx=bx):
                if ti >= 2:
                    p.wait(q, "p1st", 16 * (ti - 1))
                p.dma(q, xt1[bx][:], Xin_ap[ti * 128:(ti + 1) * 128, :], "p1xl")
            block.sync(f_xld)

            for nh in range(2):
                os_ = rot_o.next()

                def f_omm(t, qr=qr, tsub=tsub, nh=nh, os_=os_, cat_done=cat_done):
                    if tsub == 0 and nh == 0:
                        p.wait(t, "p1ct", cat_done)
                    guard(p, t, rot_o.cons[os_])
                    for hp in range(NP):
                        t.matmul(ops[os_][:], cat[:, hp * QR + tsub * 128:hp * QR + tsub * 128 + 128],
                                 wos2[:, hp * DM + nh * 512:hp * DM + (nh + 1) * 512],
                                 start=(hp == 0), stop=False)
                    p.inc(t.matmul(ops[os_][:], T["ones_big"][0:1, 0:128],
                                   bor[:, nh * 512:(nh + 1) * 512],
                                   start=False, stop=True), "p1pe")
                block.tensor(f_omm)
                rot_o.prod[os_] = ("p1pe", p.tick("p1pe"))

                def f_res(v, ti=ti, bx=bx, nh=nh, os_=os_):
                    guard(p, v, rot_o.prod[os_])
                    p.wait(v, "p1xl", 16 * (ti + 1))
                    p.inc(v.scalar_tensor_tensor(nx[bx][:, nh * 512:(nh + 1) * 512],
                                                 ops[os_][:], 1.0,
                                                 xt1[bx][:, nh * 512:(nh + 1) * 512],
                                                 ALU.mult, ALU.add), "p1rs")
                block.vector(f_res)
                rot_o.cons[os_] = ("p1rs", p.tick("p1rs"))

            def f_st(q, ti=ti, bx=bx):
                p.wait(q, "p1rs", p.tick("p1rs"))
                p.dma(q, Xout_ap[ti * 128:(ti + 1) * 128, :], nx[bx][:], "p1st")
            block.sync(f_st)

    def f_fin(g):
        p.wait(g, "p1st", p.tick("p1st"))
    block.gpsimd(f_fin)
    es_b.close()


def build_p1(xin_name="xin", xout_name="xout", debug=False):
    nc = bass.Bass("TRN2", target_bir_lowering=False)
    T = {}
    xin = nc.dram_tensor(xin_name, [N, DM], F32, kind="ExternalInput").ap()
    for w, shp in (("wq", [DM, DM]), ("wk", [DM, DM]), ("wv", [DM, DM]),
                   ("wo", [DM, DM]), ("proj", [64, 64])):
        T[w] = nc.dram_tensor(w, shp, F32, kind="ExternalInput").ap()
        T[w + "_bf"] = nc.dram_tensor(w + "_bf", shp, BF16, kind="Internal").ap()
    T["bo"] = nc.dram_tensor("bo", [DM], F32, kind="ExternalInput").ap()
    T["identity"] = nc.dram_tensor("identity", [128, 128], F32, kind="ExternalInput").ap()
    xout = nc.dram_tensor(xout_name, [N, DM], F32, kind="ExternalOutput").ap()
    dbg = None
    if debug:
        dbg = {}
        for nm, shp, dt in (("wqp", [128, 8192], BF16), ("wkp", [128, 8192], BF16),
                            ("kvall", [128, 512], BF16), ("ksall", [128, 8], BF16),
                            ("xnT", [128, 4096], BF16), ("cat", [128, 4096], BF16),
                            ("qfb", [128, 512], BF16), ("rb2", [128, 512], BF16)):
            dbg[nm] = nc.dram_tensor("dbg_" + nm, shp, dt, kind="ExternalOutput").ap()

    es = contextlib.ExitStack()
    with es:
        p = Prog(nc, es)
        T["ident_f32"] = es.enter_context(nc.sbuf_tensor("ident_f32", [128, 128], F32))
        T["ident_bf"] = es.enter_context(nc.sbuf_tensor("ident_bf", [128, 128], BF16))
        T["ones_big"] = es.enter_context(nc.sbuf_tensor("ones_big", [128, 128], BF16))
        with nc.Block() as block:
            def f_id(q):
                p.dma(q, T["ident_f32"][:], T["identity"], "idl")
            block.sync(f_id)

            def f_idc(v):
                p.wait(v, "idl", 16)
                v.tensor_copy(T["ident_bf"][:], T["ident_f32"][:])
                p.finc(v.memset(T["ones_big"][:], 1.0), v, "idc")
            block.vector(f_idc)

            def f_gate(t):
                p.wait(t, "idc", 1)
            block.tensor(f_gate)

            es2 = contextlib.ExitStack()
            with es2:
                emit_p1(p, block, es2, T, xin, xout, dbg=dbg)
    return nc



def emit_p2(p, block, es, T, Xin_ap, Xout_ap, dbg=None):
    nc = p.nc
    E = 8
    BIGNEG = -30000.0

    def sb(name, shape, dt):
        return es.enter_context(nc.sbuf_tensor("p2_" + name, shape, dt))

    xnT = sb("xnT", [128, DC * N], BF16)
    rwT = sb("rwT", [128, DC * E], BF16)
    rbr = sb("rbr", [1, E], BF16)
    cmb = sb("cmb", [128, NT * E], F32)
    ones5 = sb("ones5", [1, QR], BF16)

    es_ln = contextlib.ExitStack()

    def sbl(name, shape, dt):
        return es_ln.enter_context(nc.sbuf_tensor("p2_" + name, shape, dt))

    def psl(name, shape, dt):
        return es_ln.enter_context(nc.psum_tensor("p2_" + name, shape, dt))

    rt = sbl("rt", [128, E], F32)
    cmpall = sbl("cmpall", [128, E * E], F32)
    rnk = sbl("rnk", [128, E], F32)
    msk = sbl("msk", [128, E], mybir.dt.uint8)
    zt = sbl("zt", [128, E], F32)
    nm4 = sbl("nm4", [128, 1], F32)
    e4 = sbl("e4", [128, E], F32)
    s4c = sbl("s4c", [128, 1], F32)
    rcpc = sbl("rcpc", [128, 1], F32)
    negc = sbl("negc", [128, E], F32)

    def f_cast(g):
        p.dma(g, T["rw_bf"], T["rw"], "p2wc")
        p.dma(g, T["gub_bf"], T["gub"], "p2wc")
        p.dma(g, T["dnb_bf"], T["dnb"], "p2wc")
        for e in range(E):
            p.dma(g, T["gup_bf"][e], T["gup"][e], "p2wc")
            p.dma(g, T["dwn_bf"][e], T["dwn"][e], "p2wc")
    block.gpsimd(f_cast)
    cast_tick = {}
    for e in range(E):
        cast_tick[e] = 48 + 32 * (e + 1)

    def f_xcp(g):
        p.dma(g, Xout_ap, Xin_ap, "p2cp")
    block.gpsimd(f_xcp)

    rw_sb = sbl("rw_sb", [8, DM], BF16)
    rb32 = sbl("rb32", [1, E], F32)

    def f_rwt(sc):
        p.wait(sc, "p2wc", 16)
        p.dma(sc, rw_sb[:], T["rw_bf"], "p2rl")
        p.dma(sc, rb32[:], T["rb"].rearrange("(o e) -> o e", o=1), "p2rl")
    block.scalar(f_rwt)

    def f_pc(v):
        v.memset(negc[:], BIGNEG)
        v.memset(ones5[:], 1.0)
        p.wait(v, "p2rl", p.tick("p2rl"))
        p.finc(v.tensor_copy(rbr[:], rb32[:]), v, "p2pc")
    block.vector(f_pc)

    c = alloc_ln_common(p, sbl, psl)
    c["ident"] = T["ident_bf"]
    emit_ln(p, block, Xin_ap, xnT, c, "ln")
    ln_done = p.tick("lnev")

    rps = es_ln.enter_context(nc.psum_tensor("p2_rps", [128, E], F32))
    rtps = es_ln.enter_context(nc.psum_tensor("p2_rtps", [128, E], BF16))

    for dcc in range(DC):
        def f_rt(t, dcc=dcc):
            if dcc == 0:
                p.wait(t, "p2rl", 16)
            else:
                p.wait(t, "p2rv", p.tick("p2rv"))
            p.inc(t.transpose(rtps[:], rw_sb[:, dcc * 128:(dcc + 1) * 128],
                              T["ident_bf"][0:8, 0:8]), "p2rw")
        block.tensor(f_rt)

        def f_rtv(v, dcc=dcc):
            p.wait(v, "p2rw", p.tick("p2rw"))
            p.finc(v.tensor_copy(rwT[:, dcc * E:(dcc + 1) * E], rtps[:]), v, "p2rv")
        block.vector(f_rtv)

    rps_cons = None
    tz_ticks = []
    base_te = p.tick("p2te")
    for i in range(NT):
        def f_rmm(t, i=i):
            if i == 0:
                p.wait(t, "lnev", ln_done)
                p.wait(t, "p2rv", p.tick("p2rv"))
                p.wait(t, "p2pc", p.tick("p2pc"))
            guard(p, t, rps_cons)
            for dcc in range(DC):
                t.matmul(rps[:], xnT[:, dcc * N + i * 128:dcc * N + (i + 1) * 128],
                         rwT[:, dcc * E:(dcc + 1) * E],
                         start=(dcc == 0), stop=False)
            p.inc(t.matmul(rps[:], T["ones_big"][0:1, 0:128], rbr[:],
                           start=False, stop=True), "p2pe")
        block.tensor(f_rmm)
        rmm_tick = p.tick("p2pe")

        def f_top(v, i=i, rmm_tick=rmm_tick):
            def rt_(instr):
                n = p.inc(instr, "p2vr")
                p.wait(v, "p2vr", n)
            p.wait(v, "p2pe", rmm_tick)
            rt_(v.tensor_copy(rt[:], rps[:]))
            for e in range(E):
                rt_(v.tensor_scalar(cmpall[:, e * E:(e + 1) * E], rt[:],
                                    rt[:, e:e + 1], None, ALU.is_gt, ALU.bypass))
            for e in range(E):
                rt_(v.reduce_sum(rnk[:, e:e + 1], cmpall[:, e * E:(e + 1) * E],
                                 axis=AX.X))
            rt_(v.tensor_scalar(msk[:], rnk[:], 3.5, None, ALU.is_lt, ALU.bypass))
            rt_(v.select(zt[:], msk[:], rt[:], negc[:]))
            p.inc(v.nop(), "p2tz")
        block.vector(f_top)
        rps_cons = ("p2tz", p.tick("p2tz"))

        tz_ticks.append(p.tick("p2tz"))

        def f_cmb(v, i=i):
            def rt_(instr):
                n = p.inc(instr, "p2vr")
                p.wait(v, "p2vr", n)
            p.wait(v, "p2te", base_te + i + 1)
            rt_(v.reduce_sum(s4c[:], e4[:], axis=AX.X))
            rt_(v.reciprocal(rcpc[:], s4c[:]))
            p.inc(v.tensor_scalar(cmb[:, i * E:(i + 1) * E], e4[:], rcpc[:],
                                  1.0 / 1.702, ALU.mult, ALU.mult), "p2cm")
        block.vector(f_cmb)

    def f_rexp(sc, tz_ticks=tuple(tz_ticks)):
        for i in range(NT):
            p.wait(sc, "p2tz", tz_ticks[i])
            p.finc(sc.activation(e4[:], zt[:], AF.Exp), sc, "p2te")
    block.scalar(f_rexp)
    cmb_done = p.tick("p2cm")

    if dbg is not None:
        def f_dbg(g):
            p.wait(g, "p2cm", cmb_done)
            p.dma(g, dbg["cmb"], cmb[:], "dbgs")
            p.dma(g, dbg["xnT"], xnT[:, 0:N], "dbgs")
            p.dma(g, dbg["x1"], Xin_ap[3200:3328, :], "dbgs")
        block.gpsimd(f_dbg)

    es_ln.close()

    es_x = contextlib.ExitStack()
    gps = [es_x.enter_context(nc.psum_tensor("p2_gps%d" % k, [128, QR], F32)) for k in range(2)]
    dpsm = [es_x.enter_context(nc.psum_tensor("p2_dps%d" % k, [128, QR], F32)) for k in range(2)]
    gbt = es_x.enter_context(nc.psum_tensor("p2_gbt", [128, 32], F32))

    def sbx(name, shape, dt):
        return es_x.enter_context(nc.sbuf_tensor("p2_" + name, shape, dt))

    gupT = [sbx("gupT%d" % k, [128, DC * 2048], BF16) for k in range(2)]
    dwnT = sbx("dwnT", [128, 16 * DM], BF16)
    gub_l = sbx("gub_l", [32, 128], F32)
    gub_c = [sbx("gub_c%d" % k, [128, 32], F32) for k in range(2)]
    dnb_sb = [sbx("dnb_sb%d" % k, [1, DM], BF16) for k in range(2)]
    gte = [sbx("gte%d" % k, [128, QR], BF16) for k in range(2)]
    usb = [sbx("usb%d" % k, [128, QR], BF16) for k in range(2)]
    ssb = [sbx("ssb%d" % k, [128, QR], BF16) for k in range(2)]
    act_sb = sbx("act_sb", [128, 16 * QR], BF16)
    dx = [sbx("dx%d" % k, [128, DM], F32) for k in range(2)]

    rot_g = Rot(2)
    rot_d = Rot(2)
    rot_dx = Rot(2)
    rot_gu = Rot(2)
    rot_dw = Rot(2)
    rot_gb = Rot(2)
    acc_tick = [0 for _ in range(NT)]
    guT_tick = {}
    dwT_tick = {}
    gb_tick = {}
    ln_gate = None

    def emit_loads(e):
        s0 = rot_gu.next()
        s1 = rot_gu.next()
        sd = rot_dw.next()
        sb_ = rot_gb.next()

        def f_tl(sc, e=e, s0=s0, s1=s1, sd=sd, ln_gate=ln_gate):
            p.wait(sc, "p2wc", cast_tick[e])
            if ln_gate is not None:
                p.wait(sc, "lnev", ln_gate)
                p.wait(sc, "p2cm", p.tick("p2cm"))
            for half, sh in ((0, s0), (1, s1)):
                guard(p, sc, rot_gu.cons[sh])
                for dcc in range(DC):
                    p.dma(sc, gupT[sh][:, dcc * 2048:dcc * 2048 + 1024],
                          T["gup_bf"][e][1024 * half:1024 * half + 1024,
                                         dcc * 128:(dcc + 1) * 128],
                          "p2tl", transpose=True)
                    p.dma(sc, gupT[sh][:, dcc * 2048 + 1024:(dcc + 1) * 2048],
                          T["gup_bf"][e][2048 + 1024 * half:2048 + 1024 * half + 1024,
                                         dcc * 128:(dcc + 1) * 128],
                          "p2tl", transpose=True)
                guT_tick[(e, half)] = p.tick("p2tl")
            guard(p, sc, rot_dw.cons[sd])
            for ic in range(16):
                p.dma(sc, dwnT[:, ic * DM:(ic + 1) * DM],
                      T["dwn_bf"][e][:, ic * 128:(ic + 1) * 128], "p2tl", transpose=True)
            dwT_tick[e] = p.tick("p2tl")
        block.scalar(f_tl)

        def f_bld(q, e=e, sb_=sb_, ln_gate=ln_gate):
            p.wait(q, "p2wc", 48)
            guard(p, q, rot_gb.cons[sb_])
            if ln_gate is not None:
                p.wait(q, "lnev", ln_gate)
            p.dma(q, gub_l[:], T["gub"][e].rearrange("(c p) -> c p", p=128), "p2bl")
            p.dma(q, dnb_sb[sb_][:], T["dnb_bf"][e:e + 1, :], "p2bl")
        block.sync(f_bld)
        bl_tick = p.tick("p2bl")

        def f_gbt(t, e=e, bl_tick=bl_tick):
            p.wait(t, "p2bl", bl_tick)
            p.wait(t, "p2gb", p.tick("p2gb"))
            p.inc(t.transpose(gbt[:], gub_l[:], T["ident_f32"][0:32, 0:32]), "p2gt")
        block.tensor(f_gbt)
        gt_tick = p.tick("p2gt")

        def f_gbe(v, sb_=sb_, gt_tick=gt_tick):
            p.wait(v, "p2gt", gt_tick)
            p.finc(v.tensor_copy(gub_c[sb_][:], gbt[:]), v, "p2gb")
        block.vector(f_gbe)
        gb_tick[e] = p.tick("p2gb")
        return s0, s1, sd, sb_

    ln_gate = p.tick("lnev")
    bufs = {0: emit_loads(0)}
    ln_gate = None
    for e in range(E):
        s0, s1, sd, sb_ = bufs[e]

        for R in range(NQR):
            base_sl = p.tick("p2sl")
            gv_ticks = []
            for gp in range(16):
                sh = s0 if gp < 8 else s1
                gi = gp % 8
                sg = rot_g.next()
                su = rot_g.next()

                def f_gmm(t, e=e, R=R, gp=gp, gi=gi, sh=sh, sg=sg, su=su):
                    if R == 0 and gp == 0:
                        p.wait(t, "p2tl", guT_tick[(e, 0)])
                        p.wait(t, "p2gb", gb_tick[e])
                        if e == 0:
                            p.wait(t, "p2cm", cmb_done)
                    if R == 0 and gp == 8:
                        p.wait(t, "p2tl", guT_tick[(e, 1)])
                    for off, ss in ((gi * 128, sg), (1024 + gi * 128, su)):
                        guard(p, t, rot_g.cons[ss])
                        for dcc in range(DC):
                            instr = t.matmul(gps[ss][:],
                                             gupT[sh][:, dcc * 2048 + off:dcc * 2048 + off + 128],
                                             xnT[:, dcc * N + R * QR:dcc * N + (R + 1) * QR],
                                             start=(dcc == 0), stop=(dcc == DC - 1))
                        p.inc(instr, "p2pe")
                block.tensor(f_gmm)
                g_tick = p.tick("p2pe")
                rot_g.prod[sg] = ("p2pe", g_tick)
                rot_g.prod[su] = ("p2pe", g_tick)

                def f_gev(v, e=e, gp=gp, sg=sg, su=su, sb_=sb_, base_sl=base_sl):
                    guard(p, v, rot_g.prod[sg])
                    if gp > 0:
                        p.wait(v, "p2sl", base_sl + gp)
                    p.inc(v.tensor_scalar(gte[sg % 2][:], gps[sg][:],
                                          gub_c[sb_][:, gp:gp + 1], 7.0,
                                          ALU.add, ALU.min), "p2gv")
                    p.inc(v.tensor_scalar(usb[su % 2][:], gps[su][:],
                                          gub_c[sb_][:, 16 + gp:17 + gp], 7.0,
                                          ALU.add, ALU.min), "p2uv")
                block.vector(f_gev)
                gv_ticks.append(p.tick("p2gv"))
                uv_tick = p.tick("p2uv")
                rot_g.cons[sg] = ("p2gv", gv_ticks[-1])
                rot_g.cons[su] = ("p2uv", uv_tick)

                def f_act(v, gp=gp, sg=sg, su=su, base_sl=base_sl, uv_tick=uv_tick):
                    p.wait(v, "p2sl", base_sl + gp + 1)
                    p.wait(v, "p2uv", uv_tick)
                    v.tensor_scalar(usb[1][:], usb[1][:], -7.0, None, ALU.max, ALU.bypass)
                    p.inc(v.scalar_tensor_tensor(act_sb[:, gp * QR:(gp + 1) * QR],
                                                 usb[su % 2][:], 1.0, ssb[sg % 2][:],
                                                 ALU.add, ALU.mult), "p2av")
                block.vector(f_act)

            def f_sil(sc, gv_ticks=tuple(gv_ticks)):
                for gp in range(16):
                    p.wait(sc, "p2gv", gv_ticks[gp])
                    p.finc(sc.activation(ssb[0][:], gte[0][:], AF.Silu,
                                         scale=1.702), sc, "p2sl")
            block.scalar(f_sil)
            act_done = p.tick("p2av")
            last_gu_tick = p.tick("p2pe")

            for tsub in range(4):
                ti = R * 4 + tsub
                sdx = rot_dx.next()
                for half in range(2):
                    sdn = rot_d.next()

                    def f_dmm(t, e=e, R=R, tsub=tsub, half=half, sd2=sdn, act_done=act_done,
                              sb_=sb_):
                        if tsub == 0 and half == 0:
                            p.wait(t, "p2av", act_done)
                            if R == 0:
                                p.wait(t, "p2tl", dwT_tick[e])
                        guard(p, t, rot_d.cons[sd2])
                        for ic in range(16):
                            t.matmul(dpsm[sd2][:],
                                     act_sb[:, ic * QR + tsub * 128:ic * QR + tsub * 128 + 128],
                                     dwnT[:, ic * DM + half * 512:ic * DM + (half + 1) * 512],
                                     start=(ic == 0), stop=False)
                        p.inc(t.matmul(dpsm[sd2][:], T["ones_big"][0:1, 0:128],
                                       dnb_sb[sb_][0:1, half * 512:(half + 1) * 512],
                                       start=False, stop=True), "p2pe")
                    block.tensor(f_dmm)
                    rot_d.prod[sdn] = ("p2pe", p.tick("p2pe"))

                    def f_dev(v, e=e, ti=ti, half=half, sdn=sdn, sdx=sdx):
                        guard(p, v, rot_d.prod[sdn])
                        if half == 0:
                            guard(p, v, rot_dx.cons[sdx])
                        p.finc(v.tensor_scalar(dx[sdx][:, half * 512:(half + 1) * 512],
                                               dpsm[sdn][:], cmb[:, ti * E + e:ti * E + e + 1],
                                               None, ALU.mult, ALU.bypass), v, "p2dv")
                    block.vector(f_dev)
                    rot_d.cons[sdn] = ("p2dv", p.tick("p2dv"))

                def f_acc(g, e=e, ti=ti, sdx=sdx):
                    p.wait(g, "p2dv", p.tick("p2dv"))
                    p.wait(g, "p2cp", 16)
                    if acc_tick[ti] > 0:
                        p.wait(g, "p2ac", acc_tick[ti])
                    p.dma(g, Xout_ap[ti * 128:(ti + 1) * 128, :], dx[sdx][:],
                          "p2ac", accum_op=ALU.add)
                block.gpsimd(f_acc)
                acc_tick[ti] = p.tick("p2ac")
                rot_dx.cons[sdx] = ("p2ac", acc_tick[ti])

        end_pe = ("p2pe", p.tick("p2pe"))
        rot_gu.cons[s0] = ("p2pe", last_gu_tick)
        rot_gu.cons[s1] = ("p2pe", last_gu_tick)
        rot_dw.cons[sd] = end_pe
        rot_gb.cons[sb_] = end_pe
        if e + 1 < E:
            bufs[e + 1] = emit_loads(e + 1)

    def f_fin(g):
        p.wait(g, "p2ac", p.tick("p2ac"))
    block.gpsimd(f_fin)
    es_x.close()


def build_p2(xin_name="xin", xout_name="xout"):
    nc = bass.Bass("TRN2", target_bir_lowering=False)
    T = {}
    E = 8
    xin = nc.dram_tensor(xin_name, [N, DM], F32, kind="ExternalInput").ap()
    T["rw"] = nc.dram_tensor("rw", [E, DM], F32, kind="ExternalInput").ap()
    T["rb"] = nc.dram_tensor("rb", [E], F32, kind="ExternalInput").ap()
    T["gup"] = nc.dram_tensor("gup", [E, 4096, DM], F32, kind="ExternalInput").ap()
    T["gub"] = nc.dram_tensor("gub", [E, 4096], F32, kind="ExternalInput").ap()
    T["dwn"] = nc.dram_tensor("dwn", [E, DM, 2048], F32, kind="ExternalInput").ap()
    T["dnb"] = nc.dram_tensor("dnb", [E, DM], F32, kind="ExternalInput").ap()
    T["identity"] = nc.dram_tensor("identity", [128, 128], F32, kind="ExternalInput").ap()
    xout = nc.dram_tensor(xout_name, [N, DM], F32, kind="ExternalOutput").ap()
    T["rw_bf"] = nc.dram_tensor("rw_bf", [E, DM], BF16, kind="Internal").ap()
    T["gub_bf"] = nc.dram_tensor("gub_bf", [E, 4096], BF16, kind="Internal").ap()
    T["dnb_bf"] = nc.dram_tensor("dnb_bf", [E, DM], BF16, kind="Internal").ap()
    T["gup_bf"] = nc.dram_tensor("gup_bf", [E, 4096, DM], BF16, kind="Internal").ap()
    T["dwn_bf"] = nc.dram_tensor("dwn_bf", [E, DM, 2048], BF16, kind="Internal").ap()

    es = contextlib.ExitStack()
    with es:
        p = Prog(nc, es)
        T["ident_f32"] = es.enter_context(nc.sbuf_tensor("ident_f32", [128, 128], F32))
        T["ident_bf"] = es.enter_context(nc.sbuf_tensor("ident_bf", [128, 128], BF16))
        T["ones_big"] = es.enter_context(nc.sbuf_tensor("ones_big", [128, 128], BF16))
        with nc.Block() as block:
            def f_id(q):
                p.dma(q, T["ident_f32"][:], T["identity"], "idl")
            block.sync(f_id)

            def f_idc(v):
                p.wait(v, "idl", 16)
                v.tensor_copy(T["ident_bf"][:], T["ident_f32"][:])
                p.finc(v.memset(T["ones_big"][:], 1.0), v, "idc")
            block.vector(f_idc)

            def f_gate(t):
                p.wait(t, "idc", 1)
            block.tensor(f_gate)

            es2 = contextlib.ExitStack()
            with es2:
                emit_p2(p, block, es2, T, xin, xout)
    return nc



def emit_p3(p, block, es, T, Xin_ap, Xout_ap, dbg=None):
    nc = p.nc
    H = 16
    SLOPES = [2.0 ** (-0.5 * (h + 1)) for h in range(H)]
    SBF = [float(np.float32(BF(s))) for s in SLOPES]

    def sb(name, shape, dt):
        return es.enter_context(nc.sbuf_tensor("p3_" + name, shape, dt))

    v_sb = sb("v_sb", [128, NT * DM], BF16)

    es_q = contextlib.ExitStack()

    def sbq(name, shape, dt):
        return es_q.enter_context(nc.sbuf_tensor("p3_" + name, shape, dt))

    def psq(name, shape, dt):
        return es_q.enter_context(nc.psum_tensor("p3_" + name, shape, dt))

    xnT = sbq("xnT", [128, DC * N], BF16)
    wT = sbq("wT", [128, DC * DM], BF16)
    stg = [sbq("stg%d" % k, [128, QR], BF16) for k in range(2)]

    def f_cast(g):
        for wname in ("awq", "awk", "awv", "awo"):
            p.dma(g, T[wname + "_bf"], T[wname], "p3wc")
    block.gpsimd(f_cast)

    c = alloc_ln_common(p, lambda n, s, d: sbq(n, s, d), lambda n, s, d: psq(n, s, d))
    c["ident"] = T["ident_bf"]
    emit_ln(p, block, Xin_ap, xnT, c, "ln")
    ln_done = p.tick("lnev")

    pjs = [psq("pjs%d" % k, [128, QR], F32) for k in range(2)]

    rot_j = Rot(2)
    rot_st = Rot(2)
    for wi, wname in enumerate(("awq", "awk", "awv")):
        prev_pe = p.tick("p3pe")

        def f_tl(sc, wi=wi, wname=wname, prev_pe=prev_pe):
            p.wait(sc, "p3wc", 16 * (wi + 1))
            if wi > 0:
                p.wait(sc, "p3pe", prev_pe)
            for dcc in range(DC):
                p.dma(sc, wT[:, dcc * DM:(dcc + 1) * DM],
                      T[wname + "_bf"][:, dcc * 128:(dcc + 1) * 128], "p3tl",
                      transpose=True)
        block.scalar(f_tl)
        tl_tick = p.tick("p3tl")

        if wi < 2:
            dst = T["qt_s" if wi == 0 else "kt_s"]
            for oc in range(DC):
                for nr in range(NQR):
                    s = rot_j.next()
                    st_ = rot_st.next()

                    def f_mm(t, wi=wi, oc=oc, nr=nr, s=s, tl_tick=tl_tick):
                        if oc == 0 and nr == 0:
                            p.wait(t, "p3tl", tl_tick)
                            if wi == 0:
                                p.wait(t, "lnev", ln_done)
                        guard(p, t, rot_j.cons[s])
                        for dcc in range(DC):
                            instr = t.matmul(pjs[s][:],
                                             wT[:, dcc * DM + oc * 128:dcc * DM + (oc + 1) * 128],
                                             xnT[:, dcc * N + nr * QR:dcc * N + (nr + 1) * QR],
                                             start=(dcc == 0), stop=(dcc == DC - 1))
                        p.inc(instr, "p3pe")
                    block.tensor(f_mm)
                    rot_j.prod[s] = ("p3pe", p.tick("p3pe"))

                    def f_ev(v, wi=wi, s=s, st_=st_):
                        guard(p, v, rot_j.prod[s])
                        guard(p, v, rot_st.cons[st_])
                        p.finc(v.tensor_scalar(stg[st_][:], pjs[s][:],
                                               (0.125 if wi == 0 else 1.0), None,
                                               ALU.mult, ALU.bypass), v, "p3ev")
                    block.vector(f_ev)
                    ev_tick = p.tick("p3ev")
                    rot_j.cons[s] = ("p3ev", ev_tick)
                    rot_st.prod[st_] = ("p3ev", ev_tick)

                    def f_out(q, dst=dst, oc=oc, nr=nr, st_=st_, ev_tick=ev_tick):
                        p.wait(q, "p3ev", ev_tick)
                        p.dma(q, dst[oc * 128:(oc + 1) * 128, nr * QR:(nr + 1) * QR],
                              stg[st_][:], "p3qs")
                    block.sync(f_out)
                    rot_st.cons[st_] = ("p3qs", p.tick("p3qs"))
        else:
            for ti in range(NT):
                for half in range(2):
                    s = rot_j.next()

                    def f_mm(t, ti=ti, half=half, s=s, tl_tick=tl_tick):
                        if ti == 0 and half == 0:
                            p.wait(t, "p3tl", tl_tick)
                        guard(p, t, rot_j.cons[s])
                        for dcc in range(DC):
                            instr = t.matmul(pjs[s][:],
                                             xnT[:, dcc * N + ti * 128:dcc * N + (ti + 1) * 128],
                                             wT[:, dcc * DM + half * 512:dcc * DM + (half + 1) * 512],
                                             start=(dcc == 0), stop=(dcc == DC - 1))
                        p.inc(instr, "p3pe")
                    block.tensor(f_mm)
                    rot_j.prod[s] = ("p3pe", p.tick("p3pe"))

                    def f_ev(v, ti=ti, half=half, s=s):
                        guard(p, v, rot_j.prod[s])
                        p.inc(v.tensor_copy(
                            v_sb[:, ti * DM + half * 512:ti * DM + (half + 1) * 512],
                            pjs[s][:]), "p3vv")
                    block.vector(f_ev)
                    rot_j.cons[s] = ("p3vv", p.tick("p3vv"))

    qkv_pe = p.tick("p3pe")
    qkv_vv = p.tick("p3vv")
    qkv_qs = p.tick("p3qs")

    if dbg is not None:
        def f_dbga(g):
            p.wait(g, "p3qs", qkv_qs)
            p.wait(g, "p3vv", qkv_vv)
            p.dma(g, dbg["xnT3"], xnT[:, 0:N], "dbgs")
            p.dma(g, dbg["qt"], T["qt_s"][0:64, :], "dbgs")
            p.dma(g, dbg["kt"], T["kt_s"][0:64, :], "dbgs")
            p.dma(g, dbg["vsb"], v_sb[:, 0:DM], "dbgs")
        block.gpsimd(f_dbga)

    es_q.close()

    es_a = contextlib.ExitStack()

    def sba(name, shape, dt):
        return es_a.enter_context(nc.sbuf_tensor("p3_" + name, shape, dt))

    def psa(name, shape, dt):
        return es_a.enter_context(nc.psum_tensor("p3_" + name, shape, dt))

    ktL = sba("ktL", [69, N], BF16)
    ktH = sba("ktH", [69, N], BF16)
    rhL = sba("rhL", [69, N], BF16)
    rhH = sba("rhH", [69, N], BF16)
    vo = sba("vo", [128, NT * 65], BF16)
    ptc = [sba("ptc%d" % k, [128, 2048], BF16) for k in range(2)]
    q2s = sba("q2s", [64, QR], F32)
    ng4 = sba("ng4", [64, 1], F32)
    rbc = sba("rbc", [64, 2048], BF16)
    ost = sba("ost", [64, 2048], BF16)
    tri_sb = sba("tri_sb", [128, 16 * 128], BF16)
    trb = sba("trb", [128, 128], F32)
    dsc = sba("dsc", [65, QR], F32)
    rbf = sba("rbf", [65, 2048], BF16)

    spsum = [psa("spsum%d" % k, [128, 1024], F32) for k in range(2)]
    opsum = psa("opsum", [65, 2048], F32)

    def f_aux0(q):
        p.wait(q, "p3qs", qkv_qs)
        p.wait(q, "p3vv", qkv_vv)
        p.dma(q, tri_sb[:], T["tri"], "p3ax")
        for rh in (rhL, rhH):
            p.dma(q, rh[65:66, :], T["aux_qhi"], "p3ax")
            p.dma(q, rh[66:67, :], T["aux_qlo"], "p3ax")
        for kt in (ktL, ktH):
            p.dma(q, kt[64:65, :], T["aux_ones"], "p3ax")
            p.dma(q, kt[67:68, :], T["aux_r"], "p3ax")
            p.dma(q, kt[68:69, :], T["aux_j128"], "p3ax")
    block.sync(f_aux0)
    aux0 = p.tick("p3ax")

    def f_vm(v):
        v.memset(vo[:], 1.0)
        p.finc(v.memset(ng4[:], -4.0), v, "p3pc")
    block.vector(f_vm)

    rot_sp = Rot(2)
    rot_pt = Rot(2)
    tri_used = {}
    prev_head_pe = None
    prev_head_ct = None
    last_scores_pe = None
    for h in range(H):
        sneg = -SBF[h]
        spos = SBF[h]

        prol_pe = p.tick("p3pe")
        pl_pe = prol_pe if last_scores_pe is None else last_scores_pe

        def f_pl(q, h=h, pl_pe=pl_pe):
            if h == 0:
                p.wait(q, "p3ax", aux0)
                p.wait(q, "p3pc", p.tick("p3pc"))
            else:
                p.wait(q, "p3pe", pl_pe)
            p.dma(q, ktL[0:64, :], T["kt_s"][h * 64:(h + 1) * 64, :], "p3kl")
            p.dma(q, ktH[0:64, :], T["kt_s"][h * 64:(h + 1) * 64, :], "p3kl")
            p.dma(q, rhL[0:64, :], T["qt_s"][h * 64:(h + 1) * 64, :], "p3kl")
            p.dma(q, rhH[0:64, :], T["qt_s"][h * 64:(h + 1) * 64, :], "p3kl")
            p.dma(q, ktL[65:66, :], T["slp_pos"][h:h + 1, :], "p3kl")
            p.dma(q, ktL[66:67, :], T["slp_pos"][h:h + 1, :], "p3kl")
            p.dma(q, rhL[67:68, :], T["slp_neg"][h:h + 1, :], "p3kl")
            p.dma(q, rhL[68:69, :], T["slp_neg"][h:h + 1, :], "p3kl")
            p.dma(q, ktH[65:66, :], T["slp_neg"][h:h + 1, :], "p3kl")
            p.dma(q, ktH[66:67, :], T["slp_neg"][h:h + 1, :], "p3kl")
            p.dma(q, rhH[67:68, :], T["slp_pos"][h:h + 1, :], "p3kl")
            p.dma(q, rhH[68:69, :], T["slp_pos"][h:h + 1, :], "p3kl")
        block.gpsimd(f_pl)
        kl_tick = p.tick("p3kl")

        def f_vo(v, h=h, prol_pe=prol_pe):
            if h > 0:
                p.wait(v, "p3pe", prol_pe)
            instr = None
            for ti in range(NT):
                instr = v.tensor_copy(vo[:, ti * 65:ti * 65 + 64],
                                      v_sb[:, ti * DM + h * 64:ti * DM + (h + 1) * 64])
            p.inc(instr, "p3vo")
        block.vector(f_vo)
        vo_tick = p.tick("p3vo")

        mq_prev = None
        for qr in range(NQR):
            def f_q2(v, h=h, qr=qr, kl_tick=kl_tick, mq_prev=mq_prev):
                p.wait(v, "p3kl", kl_tick)
                if qr == 0:
                    p.wait(v, "p3pc", p.tick("p3pc"))
                guard(p, v, mq_prev)
                p.finc(v.scalar_tensor_tensor(q2s[:], rhL[0:64, qr * QR:(qr + 1) * QR],
                                              1.0, rhL[0:64, qr * QR:(qr + 1) * QR],
                                              ALU.mult, ALU.mult), v, "p3q2")
            block.vector(f_q2)
            q2_tick = p.tick("p3q2")

            def f_mq(t, qr=qr, q2_tick=q2_tick):
                p.wait(t, "p3q2", q2_tick)
                if rot_sp.cons[0] is not None:
                    guard(p, t, rot_sp.cons[0])
                p.inc(t.matmul(spsum[0][64:65, 0:QR], ng4[:], q2s[:],
                               start=True, stop=True, tile_position=(0, 64),
                               skip_group_check=True), "p3pe")
            block.tensor(f_mq)
            mq_tick = p.tick("p3pe")
            mq_prev = ("p3pe", mq_tick)

            def f_mw(v, qr=qr, mq_tick=mq_tick):
                p.wait(v, "p3pe", mq_tick)
                v.tensor_copy(rhL[64:65, qr * QR:(qr + 1) * QR], spsum[0][64:65, 0:QR])
                p.inc(v.tensor_copy(rhH[64:65, qr * QR:(qr + 1) * QR],
                                    spsum[0][64:65, 0:QR]), "p3mx")
                p.wait(v, "p3mx", p.tick("p3mx"))
                p.inc(v.nop(), "p3mw")
            block.vector(f_mw)
            rot_sp.cons[0] = ("p3mw", p.tick("p3mw"))
        mrows_tick = p.tick("p3mw")

        for g in range(2):
            op_evac = prev_head_ct
            exp_jobs = []
            base_ex = p.tick("p3ex")
            for ci in range(NT):
                for half in range(2):
                    qbase = g * 2048 + half * 1024
                    s = rot_sp.next()

                    segs = []
                    u = 0
                    while u < 8:
                        t0 = (qbase + u * 128) // 128
                        if t0 == ci:
                            segs.append((u * 128, 128, 2))
                            u += 1
                        else:
                            side = 0 if t0 < ci else 1
                            w = u
                            while w < 8 and (qbase + w * 128) // 128 != ci and \
                                  ((qbase + w * 128) // 128 < ci) == (t0 < ci):
                                w += 1
                            segs.append((u * 128, (w - u) * 128, side))
                            u = w

                    def f_smm(t, h=h, ci=ci, qbase=qbase, s=s, segs=segs,
                              kl_tick=kl_tick, mrows_tick=mrows_tick, vo_tick=vo_tick):
                        if ci == 0 and qbase % 2048 == 0:
                            p.wait(t, "p3kl", kl_tick)
                            p.wait(t, "p3mw", mrows_tick)
                            p.wait(t, "p3vo", vo_tick)
                        guard(p, t, rot_sp.cons[s])
                        instr = None
                        for off, width, side in segs:
                            kt = ktL if side == 0 else ktH
                            rh = rhL if side == 0 else rhH
                            for wo in range(0, width, 512):
                                ww = min(512, width - wo)
                                instr = t.matmul(
                                    spsum[s][:, off + wo:off + wo + ww],
                                    kt[:, ci * 128:(ci + 1) * 128],
                                    rh[:, qbase + off + wo:qbase + off + wo + ww],
                                    start=True, stop=True, skip_group_check=True)
                        p.inc(instr, "p3pe")
                    block.tensor(f_smm)
                    smm_tick = p.tick("p3pe")
                    last_scores_pe = smm_tick
                    rot_sp.prod[s] = ("p3pe", smm_tick)

                    tri_seg = [sg for sg in segs if sg[2] == 2]
                    tw_tick = None
                    if tri_seg:
                        off = tri_seg[0][0]

                        def f_tri(v, h=h, s=s, off=off, smm_tick=smm_tick):
                            p.wait(v, "p3pe", smm_tick)
                            v.tensor_copy(trb[:], spsum[s][:, off:off + 128])
                            p.inc(v.tensor_tensor(spsum[s][:, off:off + 128],
                                                  trb[:],
                                                  tri_sb[:, h * 128:(h + 1) * 128],
                                                  ALU.add), "p3tw")
                        block.vector(f_tri)
                        tw_tick = p.tick("p3tw")

                    exp_jobs.append((s, smm_tick, tw_tick,
                                     None if half != 0 else rot_pt.cons[rot_pt.idx % 2],
                                     rot_pt.idx % 2, half))
                    ex_tick = base_ex + len(exp_jobs)
                    p.cnt["p3ex"] = ex_tick
                    rot_sp.cons[s] = ("p3ex", ex_tick)

                ptidx = rot_pt.next()

                def f_pv(t, h=h, ci=ci, g=g, ptidx=ptidx, op_evac=op_evac):
                    p.wait(t, "p3ex", p.tick("p3ex"))
                    if ci == 0 and op_evac is not None:
                        guard(p, t, op_evac)
                    for qw in range(4):
                        instr = t.matmul(opsum[:, qw * QR:(qw + 1) * QR],
                                         vo[:, ci * 65:(ci + 1) * 65],
                                         ptc[ptidx][:, qw * QR:(qw + 1) * QR],
                                         start=(ci == 0), stop=(ci == NT - 1),
                                         skip_group_check=True)
                    p.inc(instr, "p3pe")
                block.tensor(f_pv)
                rot_pt.cons[ptidx] = ("p3pe", p.tick("p3pe"))
            pv_done = p.tick("p3pe")

            def f_exps(sc, jobs=tuple(exp_jobs), base_ex=base_ex, pv_done=pv_done):
                nonlocal_saved = p.cnt["p3ex"]
                p.cnt["p3ex"] = base_ex
                for s, smm_tick, tw_tick, ptguard, pts, half in jobs:
                    p.wait(sc, "p3pe", smm_tick)
                    if tw_tick is not None:
                        p.wait(sc, "p3tw", tw_tick)
                    guard(p, sc, ptguard)
                    p.finc(sc.activation(
                        ptc[pts][:, half * 1024:(half + 1) * 1024],
                        spsum[s][:], AF.Exp), sc, "p3ex")
                assert p.cnt["p3ex"] == nonlocal_saved
                p.wait(sc, "p3pe", pv_done)
                instr = None
                for qw in range(4):
                    sc.activation(dsc[64:65, 0:QR], opsum[64:65, qw * QR:(qw + 1) * QR], AF.Ln)
                    instr = sc.activation(rbf[64:65, qw * QR:(qw + 1) * QR],
                                          dsc[64:65, 0:QR], AF.Exp, scale=-1.0)
                p.finc(instr, sc, "p3dn")
            block.scalar(f_exps)
            dn_tick = p.tick("p3dn")

            def f_bc(t, dn_tick=dn_tick):
                p.wait(t, "p3dn", dn_tick)
                guard(p, t, rot_sp.cons[0])
                guard(p, t, rot_sp.cons[1])
                for qw in range(4):
                    s = qw % 2
                    instr = t.matmul(spsum[s][0:64, (qw // 2) * QR:(qw // 2) * QR + QR],
                                     T["ones_big"][64:65, 0:64],
                                     rbf[64:65, qw * QR:(qw + 1) * QR],
                                     start=True, stop=True, tile_position=(64, 0),
                                     skip_group_check=True)
                p.inc(instr, "p3pe")
            block.tensor(f_bc)
            bc_tick = p.tick("p3pe")
            rot_sp.cons[0] = None
            rot_sp.cons[1] = None

            def f_rbev(v, bc_tick=bc_tick):
                p.wait(v, "p3pe", bc_tick)
                for qw in range(4):
                    s = qw % 2
                    instr = v.tensor_copy(rbc[:, qw * QR:(qw + 1) * QR],
                                          spsum[s][0:64, (qw // 2) * QR:(qw // 2) * QR + QR])
                p.inc(instr, "p3rb")
            block.vector(f_rbev)

            def f_ct(v, h=h, g=g):
                p.wait(v, "p3rb", p.tick("p3rb"))
                p.finc(v.scalar_tensor_tensor(ost[:], opsum[0:64, :], 1.0, rbc[:],
                                              ALU.mult, ALU.mult), v, "p3ct")
            block.vector(f_ct)
            ct_tick = p.tick("p3ct")
            prev_head_ct = ("p3ct", ct_tick)
            rot_sp.cons[0] = ("p3ct", ct_tick)
            rot_sp.cons[1] = ("p3ct", ct_tick)

            def f_ot(q, h=h, g=g, ct_tick=ct_tick):
                p.wait(q, "p3ct", ct_tick)
                p.dma(q, T["ot_s"][h * 64:(h + 1) * 64, g * 2048:(g + 1) * 2048],
                      ost[:], "p3ot")
            block.sync(f_ot)
            prev_head_ct = ("p3ot", p.tick("p3ot"))

    att_pe = p.tick("p3pe")
    att_ot = p.tick("p3ot")

    if dbg is not None:
        def f_dbgb(g):
            p.wait(g, "p3ot", att_ot)
            p.dma(g, dbg["ot"], T["ot_s"][0:128, :], "dbgs")
        block.gpsimd(f_dbgb)

    es_a.close()

    es_o = contextlib.ExitStack()

    def sbo(name, shape, dt):
        return es_o.enter_context(nc.sbuf_tensor("p3_" + name, shape, dt))

    def pso(name, shape, dt):
        return es_o.enter_context(nc.psum_tensor("p3_" + name, shape, dt))

    awoT = sbo("awoT", [128, DC * DM], BF16)
    otl = [sbo("otl%d" % k, [128, DC * 128], BF16) for k in range(2)]
    bo32 = sbo("bo32", [1, DM], F32)
    bor = sbo("bor", [1, DM], BF16)
    nx = [sbo("nx%d" % k, [128, DM], F32) for k in range(2)]
    xt3 = [sbo("xt3%d" % k, [128, DM], F32) for k in range(2)]
    ops3 = [pso("ops3%d" % k, [128, QR], F32) for k in range(2)]

    def f_wol(sc):
        p.wait(sc, "p3wc", 64)
        p.wait(sc, "p3pe", att_pe)
        p.wait(sc, "p3ot", att_ot)
        for dcc in range(DC):
            p.dma(sc, awoT[:, dcc * DM:(dcc + 1) * DM],
                  T["awo_bf"][:, dcc * 128:(dcc + 1) * 128], "p3wt", transpose=True)
        p.dma(sc, bo32[:], T["abo"].rearrange("(o d) -> o d", o=1), "p3wt")
    block.scalar(f_wol)
    wol_tick = p.tick("p3wt")

    def f_boc(v):
        p.wait(v, "p3wt", wol_tick)
        p.wait(v, "p3pe", att_pe)
        p.wait(v, "p3ot", att_ot)
        p.inc(v.tensor_copy(bor[:], bo32[:]), "p3bc")
    block.vector(f_boc)

    rot_ol = Rot(2)
    rot_oo = Rot(2)
    for ti in range(NT):
        sl = rot_ol.next()

        def f_otl(q, ti=ti, sl=sl):
            if ti == 0:
                p.wait(q, "p3ot", att_ot)
            guard(p, q, rot_ol.cons[sl])
            p.dma(q, otl[sl][:].rearrange("p (c n) -> p c n", c=DC),
                  T["ot_s"].rearrange("(c p) n -> p c n", p=128)[:, :, ti * 128:(ti + 1) * 128],
                  "p3ol")
            if ti >= 2:
                p.wait(q, "p3st", 16 * (ti - 1))
            p.dma(q, xt3[ti % 2][:], Xin_ap[ti * 128:(ti + 1) * 128, :], "p3xl")
        block.sync(f_otl)
        ol_tick = p.tick("p3ol")

        for nh in range(2):
            so = rot_oo.next()

            def f_omm(t, ti=ti, sl=sl, nh=nh, so=so, ol_tick=ol_tick, wol_tick=wol_tick):
                if ti == 0 and nh == 0:
                    p.wait(t, "p3wt", wol_tick)
                    p.wait(t, "p3bc", p.tick("p3bc"))
                p.wait(t, "p3ol", ol_tick)
                guard(p, t, rot_oo.cons[so])
                for ic in range(DC):
                    t.matmul(ops3[so][:], otl[sl][:, ic * 128:(ic + 1) * 128],
                             awoT[:, ic * DM + nh * 512:ic * DM + (nh + 1) * 512],
                             start=(ic == 0), stop=False)
                p.inc(t.matmul(ops3[so][:], T["ones_big"][0:1, 0:128],
                               bor[:, nh * 512:(nh + 1) * 512],
                               start=False, stop=True), "p3pe")
            block.tensor(f_omm)
            rot_oo.prod[so] = ("p3pe", p.tick("p3pe"))

            def f_res(v, ti=ti, nh=nh, so=so):
                guard(p, v, rot_oo.prod[so])
                p.wait(v, "p3xl", 16 * (ti + 1))
                p.inc(v.scalar_tensor_tensor(nx[ti % 2][:, nh * 512:(nh + 1) * 512],
                                             ops3[so][:], 1.0,
                                             xt3[ti % 2][:, nh * 512:(nh + 1) * 512],
                                             ALU.mult, ALU.add), "p3rs")
            block.vector(f_res)
            rot_oo.cons[so] = ("p3rs", p.tick("p3rs"))
        rot_ol.cons[sl] = ("p3pe", p.tick("p3pe"))

        def f_st(q, ti=ti):
            p.wait(q, "p3rs", p.tick("p3rs"))
            p.dma(q, Xout_ap[ti * 128:(ti + 1) * 128, :], nx[ti % 2][:], "p3st")
        block.sync(f_st)

    def f_fin(g):
        p.wait(g, "p3st", p.tick("p3st"))
    block.gpsimd(f_fin)
    es_o.close()


def build_p3(xin_name="xin", xout_name="xout", debug=False):
    nc = bass.Bass("TRN2", target_bir_lowering=False)
    T = {}
    H = 16
    xin = nc.dram_tensor(xin_name, [N, DM], F32, kind="ExternalInput").ap()
    for w in ("awq", "awk", "awv", "awo"):
        T[w] = nc.dram_tensor(w, [DM, DM], F32, kind="ExternalInput").ap()
        T[w + "_bf"] = nc.dram_tensor(w + "_bf", [DM, DM], BF16, kind="Internal").ap()
    T["abo"] = nc.dram_tensor("abo", [DM], F32, kind="ExternalInput").ap()
    T["identity"] = nc.dram_tensor("identity", [128, 128], F32, kind="ExternalInput").ap()
    for nm, shp in (("aux_qhi", [1, N]), ("aux_qlo", [1, N]), ("aux_ones", [1, N]),
                    ("aux_r", [1, N]), ("aux_j128", [1, N]), ("slp_pos", [H, N]),
                    ("slp_neg", [H, N]), ("tri", [128, H * 128])):
        T[nm] = nc.dram_tensor(nm, shp, BF16, kind="ExternalInput").ap()
    xout = nc.dram_tensor(xout_name, [N, DM], F32, kind="ExternalOutput").ap()
    T["qt_s"] = nc.dram_tensor("qt_s", [DM, N], BF16, kind="Internal").ap()
    T["kt_s"] = nc.dram_tensor("kt_s", [DM, N], BF16, kind="Internal").ap()
    T["ot_s"] = nc.dram_tensor("ot_s", [DM, N], BF16, kind="Internal").ap()
    dbg = None
    if debug:
        dbg = {"xnT3": nc.dram_tensor("dbg_xnT3", [128, N], BF16, kind="ExternalOutput").ap(),
               "qt": nc.dram_tensor("dbg_qt", [64, N], BF16, kind="ExternalOutput").ap(),
               "kt": nc.dram_tensor("dbg_kt", [64, N], BF16, kind="ExternalOutput").ap(),
               "vsb": nc.dram_tensor("dbg_vsb", [128, DM], BF16, kind="ExternalOutput").ap(),
               "ot": nc.dram_tensor("dbg_ot", [128, N], BF16, kind="ExternalOutput").ap()}

    es = contextlib.ExitStack()
    with es:
        p = Prog(nc, es)
        T["ident_f32"] = es.enter_context(nc.sbuf_tensor("ident_f32", [128, 128], F32))
        T["ident_bf"] = es.enter_context(nc.sbuf_tensor("ident_bf", [128, 128], BF16))
        T["ones_big"] = es.enter_context(nc.sbuf_tensor("ones_big", [128, 128], BF16))
        with nc.Block() as block:
            def f_id(q):
                p.dma(q, T["ident_f32"][:], T["identity"], "idl")
            block.sync(f_id)

            def f_idc(v):
                p.wait(v, "idl", 16)
                v.tensor_copy(T["ident_bf"][:], T["ident_f32"][:])
                p.finc(v.memset(T["ones_big"][:], 1.0), v, "idc")
            block.vector(f_idc)

            def f_gate(t):
                p.wait(t, "idc", 1)
            block.tensor(f_gate)

            es2 = contextlib.ExitStack()
            with es2:
                emit_p3(p, block, es2, T, xin, xout, dbg=dbg)
    return nc


def p3_consts():
    H = 16
    j = np.arange(N)
    qhi = (256 * (j // 256)).astype(BF).reshape(1, N)
    qlo = (j - 256 * (j // 256)).astype(BF).reshape(1, N)
    ones_r = np.ones((1, N), BF)
    r_r = (j % 128).astype(BF).reshape(1, N)
    j128 = (128 * (j // 128)).astype(BF).reshape(1, N)
    slopes = np.array([2.0 ** (-0.5 * (h + 1)) for h in range(H)], np.float32)
    sbf = slopes.astype(BF).astype(np.float32)
    slp_pos = np.repeat(sbf.reshape(H, 1), N, axis=1).astype(BF)
    slp_neg = (-np.repeat(sbf.reshape(H, 1), N, axis=1)).astype(BF)
    r = np.arange(128).reshape(128, 1)
    u = np.arange(128).reshape(1, 128)
    tri = np.zeros((128, H * 128), np.float32)
    for h in range(H):
        tri[:, h * 128:(h + 1) * 128] = -2.0 * sbf[h] * np.maximum(r - u, 0)
    return {"aux_qhi": np.asarray(qhi), "aux_qlo": np.asarray(qlo),
            "aux_ones": np.asarray(ones_r), "aux_r": np.asarray(r_r),
            "aux_j128": np.asarray(j128),
            "slp_pos": np.asarray(slp_pos), "slp_neg": np.asarray(slp_neg),
            "tri": tri.astype(BF)}



def declare_common(nc, T):
    T["ident_bf"] = None
    return T


def build_p4(xin_name="xin", xout_name="xout"):
    nc = bass.Bass("TRN2", target_bir_lowering=False)
    T = {}
    xin = nc.dram_tensor(xin_name, [N, DM], F32, kind="ExternalInput").ap()
    T["u1"] = nc.dram_tensor("u1", [1024, 256], F32, kind="ExternalInput").ap()
    T["v1"] = nc.dram_tensor("v1", [256, 4096], F32, kind="ExternalInput").ap()
    T["b1"] = nc.dram_tensor("b1", [4096], F32, kind="ExternalInput").ap()
    T["u2"] = nc.dram_tensor("u2", [4096, 256], F32, kind="ExternalInput").ap()
    T["v2"] = nc.dram_tensor("v2", [256, 1024], F32, kind="ExternalInput").ap()
    T["b2"] = nc.dram_tensor("b2", [1024], F32, kind="ExternalInput").ap()
    T["identity"] = nc.dram_tensor("identity", [128, 128], F32, kind="ExternalInput").ap()
    xout = nc.dram_tensor(xout_name, [N, DM], F32, kind="ExternalOutput").ap()
    T["u1_bf"] = nc.dram_tensor("u1_bf", [1024, 256], BF16, kind="Internal").ap()
    T["v1_bf"] = nc.dram_tensor("v1_bf", [256, 4096], BF16, kind="Internal").ap()
    T["u2_bf"] = nc.dram_tensor("u2_bf", [4096, 256], BF16, kind="Internal").ap()
    T["v2_bf"] = nc.dram_tensor("v2_bf", [256, 1024], BF16, kind="Internal").ap()

    es = contextlib.ExitStack()
    with es:
        p = Prog(nc, es)
        T["ident_f32"] = es.enter_context(nc.sbuf_tensor("ident_f32", [128, 128], F32))
        T["ident_bf"] = es.enter_context(nc.sbuf_tensor("ident_bf", [128, 128], BF16))
        T["ones_big"] = es.enter_context(nc.sbuf_tensor("ones_big", [128, 128], BF16))
        with nc.Block() as block:
            def f_id(q):
                p.dma(q, T["ident_f32"][:], T["identity"], "idl")
            block.sync(f_id)

            def f_idc(v):
                p.wait(v, "idl", 16)
                v.tensor_copy(T["ident_bf"][:], T["ident_f32"][:])
                p.finc(v.memset(T["ones_big"][:], 1.0), v, "idc")
            block.vector(f_idc)

            def f_gate(t):
                p.wait(t, "idc", 1)
            block.tensor(f_gate)

            es2 = contextlib.ExitStack()
            with es2:
                emit_p4(p, block, es2, T, xin, xout)
    return nc



def build_fused(phases=(1, 2, 3, 4), debug2=False):
    nc = bass.Bass("TRN2", target_bir_lowering=False)
    T = {}
    H = 16
    E = 8
    xin = nc.dram_tensor("xin", [N, DM], F32, kind="ExternalInput").ap()
    xout = nc.dram_tensor("xout", [N, DM], F32, kind="ExternalOutput").ap()
    X1 = nc.dram_tensor("X1", [N, DM], F32, kind="Internal").ap()
    X2 = nc.dram_tensor("X2", [N, DM], F32, kind="Internal").ap()
    X3 = nc.dram_tensor("X3", [N, DM], F32, kind="Internal").ap()

    for w, shp in (("wq", [DM, DM]), ("wk", [DM, DM]), ("wv", [DM, DM]),
                   ("wo", [DM, DM]), ("proj", [64, 64])):
        T[w] = nc.dram_tensor(w, shp, F32, kind="ExternalInput").ap()
        T[w + "_bf"] = nc.dram_tensor(w + "_bf", shp, BF16, kind="Internal").ap()
    T["bo"] = nc.dram_tensor("bo", [DM], F32, kind="ExternalInput").ap()

    T["rw"] = nc.dram_tensor("rw", [E, DM], F32, kind="ExternalInput").ap()
    T["rb"] = nc.dram_tensor("rb", [E], F32, kind="ExternalInput").ap()
    T["gup"] = nc.dram_tensor("gup", [E, 4096, DM], F32, kind="ExternalInput").ap()
    T["gub"] = nc.dram_tensor("gub", [E, 4096], F32, kind="ExternalInput").ap()
    T["dwn"] = nc.dram_tensor("dwn", [E, DM, 2048], F32, kind="ExternalInput").ap()
    T["dnb"] = nc.dram_tensor("dnb", [E, DM], F32, kind="ExternalInput").ap()
    T["rw_bf"] = nc.dram_tensor("rw_bf", [E, DM], BF16, kind="Internal").ap()
    T["gub_bf"] = nc.dram_tensor("gub_bf", [E, 4096], BF16, kind="Internal").ap()
    T["dnb_bf"] = nc.dram_tensor("dnb_bf", [E, DM], BF16, kind="Internal").ap()
    T["gup_bf"] = nc.dram_tensor("gup_bf", [E, 4096, DM], BF16, kind="Internal").ap()
    T["dwn_bf"] = nc.dram_tensor("dwn_bf", [E, DM, 2048], BF16, kind="Internal").ap()

    for w in ("awq", "awk", "awv", "awo"):
        T[w] = nc.dram_tensor(w, [DM, DM], F32, kind="ExternalInput").ap()
        T[w + "_bf"] = nc.dram_tensor(w + "_bf", [DM, DM], BF16, kind="Internal").ap()
    T["abo"] = nc.dram_tensor("abo", [DM], F32, kind="ExternalInput").ap()
    for nm, shp in (("aux_qhi", [1, N]), ("aux_qlo", [1, N]), ("aux_ones", [1, N]),
                    ("aux_r", [1, N]), ("aux_j128", [1, N]), ("slp_pos", [H, N]),
                    ("slp_neg", [H, N]), ("tri", [128, H * 128])):
        T[nm] = nc.dram_tensor(nm, shp, BF16, kind="ExternalInput").ap()
    T["qt_s"] = nc.dram_tensor("qt_s", [DM, N], BF16, kind="Internal").ap()
    T["kt_s"] = nc.dram_tensor("kt_s", [DM, N], BF16, kind="Internal").ap()
    T["ot_s"] = nc.dram_tensor("ot_s", [DM, N], BF16, kind="Internal").ap()

    T["u1"] = nc.dram_tensor("u1", [1024, 256], F32, kind="ExternalInput").ap()
    T["v1"] = nc.dram_tensor("v1", [256, 4096], F32, kind="ExternalInput").ap()
    T["b1"] = nc.dram_tensor("b1", [4096], F32, kind="ExternalInput").ap()
    T["u2"] = nc.dram_tensor("u2", [4096, 256], F32, kind="ExternalInput").ap()
    T["v2"] = nc.dram_tensor("v2", [256, 1024], F32, kind="ExternalInput").ap()
    T["b2"] = nc.dram_tensor("b2", [1024], F32, kind="ExternalInput").ap()
    T["u1_bf"] = nc.dram_tensor("u1_bf", [1024, 256], BF16, kind="Internal").ap()
    T["v1_bf"] = nc.dram_tensor("v1_bf", [256, 4096], BF16, kind="Internal").ap()
    T["u2_bf"] = nc.dram_tensor("u2_bf", [4096, 256], BF16, kind="Internal").ap()
    T["v2_bf"] = nc.dram_tensor("v2_bf", [256, 1024], BF16, kind="Internal").ap()
    T["identity"] = nc.dram_tensor("identity", [128, 128], F32, kind="ExternalInput").ap()

    dbg2 = None
    if debug2:
        dbg2 = {"cmb": nc.dram_tensor("dbg_cmb", [128, NT * 8], F32, kind="ExternalOutput").ap(),
                "xnT": nc.dram_tensor("dbg_xnT", [128, N], BF16, kind="ExternalOutput").ap(),
                "x1": nc.dram_tensor("dbg_x1", [128, DM], F32, kind="ExternalOutput").ap()}
    dbgX = {}
    if debug2:
        for nm in ("dX1", "dX2", "dX3"):
            dbgX[nm] = nc.dram_tensor(nm, [N, DM], F32, kind="ExternalOutput").ap()

    es = contextlib.ExitStack()
    with es:
        p = Prog(nc, es)
        T["ident_f32"] = es.enter_context(nc.sbuf_tensor("ident_f32", [128, 128], F32))
        T["ident_bf"] = es.enter_context(nc.sbuf_tensor("ident_bf", [128, 128], BF16))
        T["ones_big"] = es.enter_context(nc.sbuf_tensor("ones_big", [128, 128], BF16))

        with nc.Block() as block:
            def f_id(q):
                p.dma(q, T["ident_f32"][:], T["identity"], "idl")
            block.sync(f_id)

            def f_idc(v):
                p.wait(v, "idl", 16)
                v.tensor_copy(T["ident_bf"][:], T["ident_f32"][:])
                p.finc(v.memset(T["ones_big"][:], 1.0), v, "idc")
            block.vector(f_idc)

            def f_gate(t):
                p.wait(t, "idc", 1)
            block.tensor(f_gate)

            last = max(phases)
            if 1 in phases:
                es1 = contextlib.ExitStack()
                with es1:
                    emit_p1(p, block, es1, T, xin, xout if last == 1 else X1)

        if dbgX and 1 in phases and last != 1:
            with nc.Block() as block:
                def f_d1(g):
                    p.dma(g, dbgX["dX1"], X1, "dbgx")
                    p.wait(g, "dbgx", p.tick("dbgx"))
                block.gpsimd(f_d1)

        if 2 in phases:
            with nc.Block() as block:
                es2 = contextlib.ExitStack()
                with es2:
                    emit_p2(p, block, es2, T, X1, xout if last == 2 else X2, dbg=dbg2)

        if dbgX and 2 in phases and last != 2:
            with nc.Block() as block:
                def f_d2(g):
                    p.dma(g, dbgX["dX2"], X2, "dbgx")
                    p.wait(g, "dbgx", p.tick("dbgx"))
                block.gpsimd(f_d2)

        if 3 in phases:
            with nc.Block() as block:
                es3 = contextlib.ExitStack()
                with es3:
                    emit_p3(p, block, es3, T, X2, xout if last == 3 else X3)

        if dbgX and 3 in phases and last != 3:
            with nc.Block() as block:
                def f_d3(g):
                    p.dma(g, dbgX["dX3"], X3, "dbgx")
                    p.wait(g, "dbgx", p.tick("dbgx"))
                block.gpsimd(f_d3)

        if 4 in phases:
            with nc.Block() as block:
                es4 = contextlib.ExitStack()
                with es4:
                    emit_p4(p, block, es4, T, X3, xout)
    return nc


def kernel(**inputs):
    x = np.ascontiguousarray(inputs["L0_attn_x"]).astype(np.float32, copy=False)
    ident = np.eye(128, dtype=np.float32)
    c3 = p3_consts()
    core_ids = list(range(NCORES))
    nc = build_fused()
    maps = []
    for b in range(NCORES):
        m = {"xin": x[b], "identity": ident,
             "wq": inputs["L0_attn_Wq"], "wk": inputs["L0_attn_Wk"],
             "wv": inputs["L0_attn_Wv"], "proj": inputs["L0_attn_proj"],
             "wo": inputs["L0_attn_Wo"], "bo": inputs["L0_attn_bo"],
             "rw": inputs["L0_ffn_router_weight"], "rb": inputs["L0_ffn_router_bias"],
             "gup": inputs["L0_ffn_gate_up_proj"], "gub": inputs["L0_ffn_gate_up_proj_bias"],
             "dwn": inputs["L0_ffn_down_proj"], "dnb": inputs["L0_ffn_down_proj_bias"],
             "awq": inputs["L1_attn_Wq"], "awk": inputs["L1_attn_Wk"],
             "awv": inputs["L1_attn_Wv"], "awo": inputs["L1_attn_Wo"],
             "abo": inputs["L1_attn_bo"],
             "u1": inputs["L1_ffn_U1"], "v1": inputs["L1_ffn_V1"], "b1": inputs["L1_ffn_b1"],
             "u2": inputs["L1_ffn_U2"], "v2": inputs["L1_ffn_V2"], "b2": inputs["L1_ffn_b2"]}
        m.update(c3)
        maps.append(m)
    r = run_bass_kernel_spmd(nc, maps, core_ids)
    return np.stack([np.asarray(rr["xout"]) for rr in r.results], axis=0)


def kernel_unfused(**inputs):
    x = np.ascontiguousarray(inputs["L0_attn_x"]).astype(np.float32, copy=False)
    ident = np.eye(128, dtype=np.float32)
    c3 = p3_consts()
    core_ids = list(range(NCORES))

    nc1 = build_p1()
    maps = [{"xin": x[b], "wq": inputs["L0_attn_Wq"], "wk": inputs["L0_attn_Wk"],
             "wv": inputs["L0_attn_Wv"], "proj": inputs["L0_attn_proj"],
             "wo": inputs["L0_attn_Wo"], "bo": inputs["L0_attn_bo"],
             "identity": ident} for b in range(NCORES)]
    r = run_bass_kernel_spmd(nc1, maps, core_ids)
    X1 = [np.asarray(rr["xout"]) for rr in r.results]

    nc2 = build_p2()
    maps = [{"xin": X1[b], "rw": inputs["L0_ffn_router_weight"],
             "rb": inputs["L0_ffn_router_bias"], "gup": inputs["L0_ffn_gate_up_proj"],
             "gub": inputs["L0_ffn_gate_up_proj_bias"], "dwn": inputs["L0_ffn_down_proj"],
             "dnb": inputs["L0_ffn_down_proj_bias"], "identity": ident}
            for b in range(NCORES)]
    r = run_bass_kernel_spmd(nc2, maps, core_ids)
    X2 = [np.asarray(rr["xout"]) for rr in r.results]

    nc3 = build_p3()
    maps = []
    for b in range(NCORES):
        m = {"xin": X2[b], "awq": inputs["L1_attn_Wq"], "awk": inputs["L1_attn_Wk"],
             "awv": inputs["L1_attn_Wv"], "awo": inputs["L1_attn_Wo"],
             "abo": inputs["L1_attn_bo"], "identity": ident}
        m.update(c3)
        maps.append(m)
    r = run_bass_kernel_spmd(nc3, maps, core_ids)
    X3 = [np.asarray(rr["xout"]) for rr in r.results]

    nc4 = build_p4()
    maps = [{"xin": X3[b], "u1": inputs["L1_ffn_U1"], "v1": inputs["L1_ffn_V1"],
             "b1": inputs["L1_ffn_b1"], "u2": inputs["L1_ffn_U2"],
             "v2": inputs["L1_ffn_V2"], "b2": inputs["L1_ffn_b2"],
             "identity": ident} for b in range(NCORES)]
    r = run_bass_kernel_spmd(nc4, maps, core_ids)
    out = np.stack([np.asarray(rr["xout"]) for rr in r.results], axis=0)
    return out
